# version 10
# speedup vs baseline: 1.1178x; 1.0039x over previous
import numpy as np
import math
from contextlib import ExitStack
import concourse.bass as bass
import concourse.mybir as mybir
from concourse.bass_utils import run_bass_kernel_spmd

F32 = mybir.dt.float32
BF16 = mybir.dt.bfloat16
I32 = mybir.dt.int32
U32 = mybir.dt.uint32
AF = mybir.ActivationFunctionType
ALU = mybir.AluOpType
AX = mybir.AxisListType

ENGINES = ("sync", "tensor", "vector", "scalar", "gpsimd")
NCORES = 8
B_CORE = 4
T = 2048
N = B_CORE * T
NGLOBAL = N * NCORES



class Sem:
    __slots__ = ("name", "h", "n")

    def __init__(self, name, handle):
        self.name = name
        self.h = handle
        self.n = 0

    def inc(self, k):
        self.n += k
        return self.n


class Prog:
    def __init__(self, nc):
        self.nc = nc
        self.es = ExitStack()
        self.streams = {e: [] for e in ENGINES}

    def sb(self, name, shape, dt):
        return self.es.enter_context(self.nc.sbuf_tensor(name, shape, dt))

    def psum(self, name, shape, dt=F32):
        return self.es.enter_context(self.nc.psum_tensor(name, shape, dt))

    def sem(self, name):
        return Sem(name, self.es.enter_context(self.nc.semaphore(name)))

    def op(self, engine, fn, inc=None, wait=None):
        waits = [] if wait is None else ([wait] if isinstance(wait, tuple) else list(wait))
        incs = [] if inc is None else ([inc] if isinstance(inc, tuple) else list(inc))
        inc_resolved = [(s, k, s.inc(k)) for (s, k) in incs]

        def emit(eng):
            for (s, v) in waits:
                if v > 0:
                    eng.wait_ge(s.h, v)
            ins = fn(eng)
            for (s, k, _) in inc_resolved:
                ins.then_inc(s.h, k)
        self.streams[engine].append(emit)

    def wait(self, engine, sem, value):
        if value > 0:
            self.streams[engine].append(lambda eng, s=sem, v=value: eng.wait_ge(s.h, v))

    def build(self):
        with self.es:
            block = self.es.enter_context(self.nc.Block())
            for name in ENGINES:
                stream = self.streams[name]
                if not stream:
                    continue

                def section(eng, stream=stream):
                    for emit in stream:
                        emit(eng)
                getattr(block, name)(section)
        return self.nc


class MK:
    def __init__(self, nc, B, Tl):
        self.nc = nc
        self.B = B
        self.T = Tl
        self.N = B * Tl
        self.NT = self.N // 128
        self.TT = Tl // 128
        self.NB512 = self.N // 512
        self.TB512 = Tl // 512
        p = Prog(nc)
        self.p = p
        self.lsx = [p.sem("s_lsx0"), p.sem("s_lsx1")]
        self.lst = p.sem("s_lst")
        self.lst2 = [p.sem("s_lst2a"), p.sem("s_lst2b")]
        self.lsb = p.sem("s_lsb")
        self.lsto = p.sem("s_lsto")
        self.ld = p.sem("s_ld")
        self.mm = p.sem("s_mm")
        self.sc = p.sem("s_sc")
        self.vv = p.sem("s_vv")
        self.gp = p.sem("s_gp")
        self.all_sems = [*self.lsx, self.lst, *self.lst2, self.lsb, self.lsto, self.ld,
                         self.mm, self.sc, self.vv, self.gp]

        self.ones_bf = p.sb("c_ones_bf", [128, 128], BF16)
        self.ones32 = p.sb("c_ones32", [128, 128], F32)
        self.ident = p.sb("c_ident", [128, 128], BF16)
        self.ident32 = p.sb("c_ident32", [128, 128], F32)

        self.xt = p.sb("b_xt", [128, 2, 1024], F32)
        self.sq2 = p.sb("b_sq2", [128, 2048], F32)
        self.xsq = self.sq2.rearrange("p (a b) -> p a b", a=2)[:, 0, :]
        self.uni = p.sb("b_uni", [128, 4096], BF16)
        self.xo = self.uni.bitcast(F32).rearrange("p (a b) -> p a b", a=2)
        self.xln = self.uni[:, 0:2048].rearrange("p (a b) -> p a b", a=2)
        self.ysc = self.uni.bitcast(F32)[:, 0:1024].rearrange("p (a b) -> p a b", a=2)
        self.cols = p.sb("b_cols", [128, 16], F32)
        self.nbias = p.sb("b_nbias", [128, 1], F32)
        self.zrow = p.sb("b_zrow", [128, 2, 512], F32)
        self.srow = p.sb("b_srow", [128, 64], F32)
        self.bcols = p.sb("b_bcols", [128, 16], F32)
        self.epscol = p.sb("b_epscol", [128, 1], F32)
        self.pihalf = p.sb("b_pihalf", [128, 1], F32)
        self.ohbuf = p.sb("b_oh1", [128, self.N // 128, 8], BF16)
        self.pcolb = p.sb("b_pcol", [128, self.N // 128], F32)
        self.scaleb = p.sb("b_scale", [128, self.N // 128], F32)
        self.su32 = p.sb("b_su32", [128, self.N // 128], U32)
        self.ohTc = p.sb("b_ohTc", [8, 1024], F32)
        self.bb = p.sb("b_bb", [128, 64], F32)
        self.bbu = p.sb("b_bbu", [128, 48], U32)
        self.b1c = p.sb("b_b1c", [128, 32], BF16)
        self.b2r = p.sb("b_b2r", [128, 512], BF16)
        self._lazy = {"biasrows": ([1, 4, 1024], BF16),
                      "rtw": ([128, 8, 8], BF16)}
        self._xlt2 = p.sb("b_xlt", [128, 2, 8, 512], BF16)
        self._xlt_base = self._xlt2[:, 0, :, :]
        self.xlt = self._xlt_base
        Tq = 8 * Tl
        self._oA = {"qt": 0, "kt": Tq, "vt": 2 * Tq, "pt": 3 * Tq,
                    "lat": 3 * Tq + 2048, "cos": 3 * Tq + 2048 + Tl,
                    "wA": 3 * Tq + 4096 + Tl}
        SBS0 = min(2048, self.N)
        f0sz = 8 * SBS0 + 16 * SBS0 + 4096 + (self.N // 128) * 8
        asz = max(self._oA["wA"] + 16384, 73728, f0sz)
        self.arena = p.sb("b_arena", [128, asz], BF16)
        self.phase_attn()
        self.ps = [p.psum(f"ps{i}", [128, 512]) for i in range(8)]
        self._init_consts()

    def _v(self, off, nelem):
        return self.arena[:, off:off + nelem]

    def phase_attn(self):
        self.xlt = self._xlt_base
        o = self._oA
        Tl = self.T
        self.qt = self._v(o["qt"], 8 * Tl).rearrange("p (c t) -> p c t", c=8)
        self.kt = self._v(o["kt"], 8 * Tl).rearrange("p (c t) -> p c t", c=8)
        self.vt = self._v(o["vt"], 8 * Tl).rearrange("p (c t) -> p c t", c=self.TT)
        self.pt = self._v(o["pt"], 2048).rearrange("p (c t) -> p c t", c=4)
        self.lnat = self._v(o["pt"], 2048).rearrange("p (c t) -> p c t", c=2)
        self.latkv = self._v(o["lat"], Tl)
        self.cossin = self._v(o["cos"], 2048).bitcast(F32).rearrange("p (a b c) -> p a b c", a=2, b=16)
        self.wA = self._v(o["wA"], 8192).rearrange("p (c t) -> p c t", c=8)
        self.wB = self._v(o["wA"] + 8192, 8192).rearrange("p (c t) -> p c t", c=8)

    def phase_f0(self):
        SBS = min(2048, self.N)
        o = 8 * SBS
        self.xlt2048 = self._v(0, 8 * SBS).rearrange("p (c t) -> p c t", c=8)
        self.acc = self._v(o, 16 * SBS).bitcast(F32).rearrange("p (c t) -> p c t", c=SBS // 128)
        self.hbuf = self._v(o + 16 * SBS, 4096).rearrange("p (c t) -> p c t", c=8)
        self.lnf0 = self._v(o + 16 * SBS, 2048).rearrange("p (c t) -> p c t", c=2)
        self.selbuf = self._v(o + 16 * SBS + 4096, self.NT * 8).rearrange("p (c t) -> p c t", c=self.NT)
        ow = self._oA["wA"]
        self.wA = self._v(ow, 8192).rearrange("p (c t) -> p c t", c=8)
        self.wB = self._v(ow + 8192, 8192).rearrange("p (c t) -> p c t", c=8)

    def phase_f1(self):
        self.xlt = self._xlt_base
        self.w1g = self._v(0, 32768).rearrange("p (c t) -> p c t", c=8)
        self.w2g = self._v(32768, 32768).rearrange("p (c t) -> p c t", c=32)
        self.hT = self._v(65536, 8192).rearrange("p (c t) -> p c t", c=32)
        self.lnf1 = self._v(65536, 2048).rearrange("p (c t) -> p c t", c=2)

    def __getattr__(self, name):
        lazy = object.__getattribute__(self, "__dict__").get("_lazy")
        if lazy and name in lazy:
            shape, dt = lazy[name]
            buf = self.p.sb("b_" + name, shape, dt)
            setattr(self, name, buf)
            return buf
        raise AttributeError(name)

    def _init_consts(self):
        p = self.p
        p.op("gpsimd", lambda g: g.memset(self.ones_bf[:, :], 1.0), inc=(self.gp, 1))
        p.op("gpsimd", lambda g: g.memset(self.ones32[:, :], 1.0), inc=(self.gp, 1))
        p.op("gpsimd", lambda g: g.memset(self.epscol[:, :], 1e-5), inc=(self.gp, 1))
        p.op("gpsimd", lambda g: g.memset(self.pihalf[:, :], 1.5707963267948966), inc=(self.gp, 1))
        p.op("gpsimd", lambda g: g.affine_select(self.ident[:, :], self.ones_bf[:, :], pattern=[[1, 128]],
                                                 compare_op=ALU.is_equal, fill=0.0, base=0,
                                                 channel_multiplier=-1),
             wait=(self.gp, self.gp.n), inc=(self.gp, 1))
        p.op("gpsimd", lambda g: g.affine_select(self.ident32[:, :], self.ones32[:, :], pattern=[[1, 128]],
                                                 compare_op=ALU.is_equal, fill=0.0, base=0,
                                                 channel_multiplier=-1),
             wait=(self.gp, self.gp.n), inc=(self.gp, 1))

    def load_wT(self, dst, wD, nelem, sem=None, wait=None):
        sem = sem or self.ld
        df = dst.rearrange("p c t -> p (c t)")
        h = nelem // 2
        self.p.op("gpsimd", lambda g: g.dma_start(out=df[:, 0:h], in_=wD[:, 0:h]),
                  inc=(sem, 16), wait=wait)
        self.p.op("gpsimd", lambda g: g.dma_start(out=df[:, h:nelem], in_=wD[:, h:nelem]),
                  inc=(sem, 16), wait=wait)




def fence(mk, engine):
    for s in mk.all_sems:
        mk.p.wait(engine, s, s.n)


def fence_all(mk):
    for e in ("tensor", "vector", "scalar", "gpsimd", "sync"):
        fence(mk, e)


def ln_stream(mk, xD, xlnD):
    p = mk.p
    for i in range(mk.NT):
        sl = i % 2
        p.op("sync", lambda s, i=i, sl=sl: s.dma_start(out=mk.xt[:, sl, :], in_=xD[i * 128:(i + 1) * 128, :]),
             wait=[(mk.vv, mk.vv.n), (mk.sc, mk.sc.n)], inc=(mk.lsx[sl], 16))
        ln_tile(mk, mk.xt[:, sl, :], mk.xln[:, sl, :], (mk.lsx[sl], mk.lsx[sl].n),
                store_war=(mk.lsto, mk.lsto.n))
        p.op("sync", lambda s, i=i, sl=sl: s.dma_start(out=xlnD[i * 128:(i + 1) * 128, :], in_=mk.xln[:, sl, :]),
             wait=(mk.vv, mk.vv.n), inc=(mk.lsto, 16))


def ln_tile(mk, xin, xout, wait_load, store_war=None):
    p = mk.p
    c = mk.cols
    p.op("vector", lambda v: v.reduce_sum(c[:, 0:1], xin, axis=AX.X), wait=[wait_load], inc=(mk.vv, 1))
    p.op("scalar", lambda s: s.activation(out=mk.xsq[:, :], in_=xin, func=AF.Square,
                                          accum_out=c[:, 1:2]),
         wait=[wait_load, (mk.sc, mk.sc.n)], inc=(mk.sc, 1))
    p.op("vector", lambda v: v.tensor_scalar(c[:, 2:3], c[:, 0:1], 1.0 / 1024, None, ALU.mult),
         wait=(mk.vv, mk.vv.n), inc=(mk.vv, 1))
    p.op("vector", lambda v: v.tensor_tensor(c[:, 3:4], c[:, 2:3], c[:, 2:3], ALU.mult),
         wait=(mk.vv, mk.vv.n), inc=(mk.vv, 1))
    p.op("vector", lambda v: v.scalar_tensor_tensor(c[:, 4:5], c[:, 1:2], 1.0 / 1024, c[:, 3:4],
                                                    ALU.mult, ALU.subtract),
         wait=[(mk.sc, mk.sc.n), (mk.vv, mk.vv.n)], inc=(mk.vv, 1))
    p.op("scalar", lambda s: s.activation(out=c[:, 5:6], in_=c[:, 4:5], func=AF.Sqrt, bias=mk.epscol[:, :]),
         wait=(mk.vv, mk.vv.n), inc=(mk.sc, 1))
    p.op("vector", lambda v: v.reciprocal(c[:, 6:7], c[:, 5:6]), wait=(mk.sc, mk.sc.n), inc=(mk.vv, 1))
    p.op("vector", lambda v: v.tensor_scalar(xout, xin, c[:, 2:3], c[:, 6:7], ALU.subtract, ALU.mult),
         wait=[(mk.vv, mk.vv.n)] + ([store_war] if store_war else []), inc=(mk.vv, 1))


def load_xlt(mk, xlnD, seq, blk, slot):
    p = mk.p
    sl = slot % 2
    dst = mk._xlt2[:, sl, :, :]
    sem = mk.lst2[sl]
    base = seq * mk.T + blk * 512
    for c in range(8):
        p.op("sync", lambda s, c=c, base=base, dst=dst: s.dma_start(
            out=dst[:, c, :],
            in_=xlnD[base:base + 512, c * 128:(c + 1) * 128], transpose=True),
            wait=[(mk.lsto, mk.lsto.n), (mk.mm, mk.mm.n)] if c == 0 else None,
            inc=(sem, 16))
    return dst, sem, sem.n


def outproj_residual(mk, ynT, woT, borow, xD, xoutD, seq, xlnoutD=None):
    p = mk.p
    for tt in range(mk.TT):
        gtile = seq * mk.TT + tt
        sl = tt % 2
        p.op("sync", lambda s, g=gtile, sl=sl: s.dma_start(out=mk.xt[:, sl, :], in_=xD[g * 128:(g + 1) * 128, :]),
             wait=[(mk.vv, mk.vv.n), (mk.sc, mk.sc.n)], inc=(mk.lsx[sl], 16))
        ld_n = mk.lsx[sl].n
        for oh in range(2):
            bank = mk.ps[oh]
            for ci in range(8):
                p.op("tensor", lambda t, ci=ci, oh=oh, tt=tt, bank=bank: t.matmul(
                    bank[:, :], ynT[:, ci, tt * 128:(tt + 1) * 128], woT[:, ci, oh * 512:(oh + 1) * 512],
                    start=(ci == 0), stop=False),
                    wait=[(mk.vv, mk.vv.n)] if (ci == 0 and oh == 0) else None)
            p.op("tensor", lambda t, oh=oh, bank=bank: t.matmul(
                bank[:, :], mk.ones_bf[0:1, :], borow[0:1, oh * 512:(oh + 1) * 512],
                start=False, stop=True), inc=(mk.mm, 1))
        mm_n = mk.mm.n
        for oh in range(2):
            p.op("vector", lambda v, oh=oh, sl=sl: v.tensor_tensor(
                mk.xo[:, sl, oh * 512:(oh + 1) * 512], mk.ps[oh][:, :],
                mk.xt[:, sl, oh * 512:(oh + 1) * 512], ALU.add),
                wait=[(mk.mm, mm_n), (mk.lsx[sl], ld_n), (mk.lsto, mk.lsto.n)], inc=(mk.vv, 1))
        p.op("sync", lambda s, g=gtile, sl=sl: s.dma_start(out=xoutD[g * 128:(g + 1) * 128, :], in_=mk.xo[:, sl, :]),
             wait=(mk.vv, mk.vv.n), inc=(mk.lsto, 16))
        if xlnoutD is not None:
            ln_tile(mk, mk.xo[:, sl, :], mk.lnat[:, sl, :], (mk.vv, mk.vv.n),
                    store_war=(mk.lsto, mk.lsto.n))
            p.op("sync", lambda s, g=gtile, sl=sl: s.dma_start(
                out=xlnoutD[g * 128:(g + 1) * 128, :], in_=mk.lnat[:, sl, :]),
                wait=(mk.vv, mk.vv.n), inc=(mk.lsto, 16))


def bcast_col(mk, src11, dst_col, psbank):
    p = mk.p
    p.op("tensor", lambda t: t.matmul(psbank[:, 0:1], mk.ones32[0:1, 0:128], src11,
                                      start=True, stop=True), wait=(mk.vv, mk.vv.n), inc=(mk.mm, 1))
    p.op("vector", lambda v: v.tensor_copy(dst_col, psbank[:, 0:1]), wait=(mk.mm, mk.mm.n), inc=(mk.vv, 1))


def _brow32(mk):
    return mk.biasrows.bitcast(F32).rearrange("a c t -> a (c t)")[0:1, 1024:2048]


def load_bias_row32(mk, slot, bD):
    mk.p.op("sync", lambda s: s.dma_start(out=_brow32(mk), in_=bD[None, :]),
            wait=(mk.mm, mk.mm.n), inc=(mk.lsb, 16))


def bias_cols_compute(mk, slot, dstcols, psbank):
    p = mk.p
    ls_n = mk.lsb.n
    row = _brow32(mk)
    for c in range(8):
        p.op("tensor", lambda t, c=c, row=row: t.matmul(psbank[:, c:c + 1], row[0:1, c * 128:(c + 1) * 128],
                                                        mk.ones32[0:1, 0:1], start=True, stop=True),
             wait=[(mk.lsb, ls_n), (mk.vv, mk.vv.n), (mk.gp, mk.gp.n)] if c == 0 else None,
             inc=(mk.mm, 1) if c == 7 else None)
    p.op("vector", lambda v: v.tensor_copy(dstcols, psbank[:, 0:8]), wait=(mk.mm, mk.mm.n), inc=(mk.vv, 1))


def load_bias(mk, j, bD):
    mk.p.op("gpsimd", lambda g: g.dma_start(out=mk.biasrows[0:1, j, :], in_=bD[None, :]), inc=(mk.ld, 16))
    return mk.biasrows[0:1, j, :]




def reload_w(mk, dst, wD, nelem=8192):
    df = dst.rearrange("p c t -> p (c t)")
    w_waits = [(mk.mm, mk.mm.n), (mk.sc, mk.sc.n), (mk.vv, mk.vv.n)]
    h = nelem // 2
    mk.p.op("gpsimd", lambda g: g.dma_start(out=df[:, 0:h], in_=wD[:, 0:h]),
            wait=list(w_waits), inc=(mk.ld, 16))
    mk.p.op("gpsimd", lambda g: g.dma_start(out=df[:, h:nelem], in_=wD[:, h:nelem]),
            wait=list(w_waits), inc=(mk.ld, 16))
    mk.p.wait("tensor", mk.ld, mk.ld.n)


def qk_fm_seq(mk, xlnD, seq, bqcol=None, bkcol=None):
    p = mk.p
    cur = load_xlt(mk, xlnD, seq, 0, 0)
    for blk in range(mk.TB512):
        xl, sem, ld_n = cur
        if blk + 1 < mk.TB512:
            cur = load_xlt(mk, xlnD, seq, blk + 1, blk + 1)
        first = True
        for (w, dst, bcol) in ((mk.wA, mk.qt, bqcol), (mk.wB, mk.kt, bkcol)):
            for oc in range(8):
                bank = mk.ps[oc % 2]
                for ci in range(8):
                    p.op("tensor", lambda t, ci=ci, oc=oc, w=w, bank=bank, xl=xl: t.matmul(
                        bank[:, :], w[:, ci, oc * 128:(oc + 1) * 128], xl[:, ci, :],
                        start=(ci == 0), stop=(ci == 7)),
                        wait=[(sem, ld_n), (mk.sc, mk.sc.n)] if (first and ci == 0) else (
                            (mk.sc, mk.sc.n) if ci == 0 else None),
                        inc=(mk.mm, 1) if ci == 7 else None)
                first = False
                if bcol is not None:
                    p.op("scalar", lambda s, oc=oc, blk=blk, dst=dst, bank=bank, bcol=bcol: s.activation(
                        out=dst[:, oc, blk * 512:(blk + 1) * 512], in_=bank[:, :], func=AF.Identity,
                        bias=bcol[:, oc:oc + 1]), wait=(mk.mm, mk.mm.n), inc=(mk.sc, 1))
                else:
                    p.op("scalar", lambda s, oc=oc, blk=blk, dst=dst, bank=bank: s.activation(
                        out=dst[:, oc, blk * 512:(blk + 1) * 512], in_=bank[:, :], func=AF.Identity),
                        wait=(mk.mm, mk.mm.n), inc=(mk.sc, 1))


def v_tm_seq(mk, xlnD, seq, wv, bvrow=None):
    p = mk.p
    cur = load_xlt(mk, xlnD, seq, 0, 0)
    for blk in range(mk.TB512):
        xl, sem, ld_n = cur
        if blk + 1 < mk.TB512:
            cur = load_xlt(mk, xlnD, seq, blk + 1, blk + 1)
        for tt4 in range(4):
            ttile = blk * 4 + tt4
            for oh in range(2):
                bank = mk.ps[2 + oh]
                for ci in range(8):
                    p.op("tensor", lambda t, ci=ci, oh=oh, tt4=tt4, bank=bank, wv=wv, xl=xl: t.matmul(
                        bank[:, :], xl[:, ci, tt4 * 128:(tt4 + 1) * 128],
                        wv[:, ci, oh * 512:(oh + 1) * 512], start=(ci == 0),
                        stop=(bvrow is None and ci == 7)),
                        wait=[(sem, ld_n), (mk.sc, mk.sc.n)] if ci == 0 else None,
                        inc=(mk.mm, 1) if (bvrow is None and ci == 7) else None)
                if bvrow is not None:
                    p.op("tensor", lambda t, oh=oh, bank=bank: t.matmul(
                        bank[:, :], mk.ones_bf[0:1, :], bvrow[:, oh * 512:(oh + 1) * 512],
                        start=False, stop=True), inc=(mk.mm, 1))
            mm_n = mk.mm.n
            for oh in range(2):
                p.op("scalar", lambda s, oh=oh, ttile=ttile: s.activation(
                    out=mk.vt[:, ttile, oh * 512:(oh + 1) * 512], in_=mk.ps[2 + oh][:, :], func=AF.Identity),
                    wait=(mk.mm, mm_n), inc=(mk.sc, 1))

def sub_A0a(mk, xD, xlnD, wbT, wvT, bvD, woT, boD, xoutD, xlnoutD=None):
    p = mk.p
    mk.phase_attn()
    T, TT = mk.T, mk.TT
    mk.load_wT(mk.wA, wbT, 512)
    mk.load_wT(mk.wB, wvT, 8192)
    bvrow = load_bias(mk, 0, bvD)
    borow = load_bias(mk, 1, boD)
    ld_w = mk.ld.n
    fence_all(mk)
    ln_stream(mk, xD, xlnD)
    wbf = mk.wA.rearrange("p c t -> p (c t)")
    lat = mk.kt.rearrange("p c t -> p (c t)")
    strip = mk.kt.rearrange("p c t -> p (c t)").rearrange("p (n f) -> p n f", f=512)
    soff = 4 * mk.T // 512
    sq = mk.sq2

    for seq in range(mk.B):
        fence_all(mk)
        p.wait("tensor", mk.ld, ld_w)
        cur = load_xlt(mk, xlnD, seq, 0, 0)
        for blk in range(mk.TB512):
            xl, sem, ld_n = cur
            if blk + 1 < mk.TB512:
                cur = load_xlt(mk, xlnD, seq, blk + 1, blk + 1)
            for half in range(2):
                pos = (0, 0) if half == 0 else (0, 64)
                rows = slice(0, 64) if half == 0 else slice(64, 128)
                for ci in range(8):
                    p.op("tensor", lambda t, ci=ci, rows=rows, pos=pos, xl=xl: t.matmul(
                        mk.ps[0][rows, :], wbf[:, ci * 64:(ci + 1) * 64], xl[:, ci, :],
                        start=(ci == 0), stop=(ci == 7), tile_position=pos),
                        wait=[(sem, ld_n), (mk.sc, mk.sc.n)] if (ci == 0 and half == 0) else None,
                        inc=(mk.mm, 1) if (ci == 7 and half == 1) else None)
            p.op("scalar", lambda s, blk=blk: s.activation(out=lat[:, blk * 512:(blk + 1) * 512],
                                                           in_=mk.ps[0][:, :], func=AF.Identity),
                 wait=(mk.mm, mk.mm.n), inc=(mk.sc, 1))
            for tt4 in range(4):
                ttile = blk * 4 + tt4
                for oh in range(2):
                    bank = mk.ps[2 + oh]
                    for ci in range(8):
                        p.op("tensor", lambda t, ci=ci, oh=oh, tt4=tt4, bank=bank, xl=xl: t.matmul(
                            bank[:, :], xl[:, ci, tt4 * 128:(tt4 + 1) * 128],
                            mk.wB[:, ci, oh * 512:(oh + 1) * 512], start=(ci == 0), stop=False),
                            wait=(mk.sc, mk.sc.n) if ci == 0 else None)
                    p.op("tensor", lambda t, oh=oh, bank=bank: t.matmul(
                        bank[:, :], mk.ones_bf[0:1, :], bvrow[:, oh * 512:(oh + 1) * 512],
                        start=False, stop=True), inc=(mk.mm, 1))
                mm_n = mk.mm.n
                for oh in range(2):
                    p.op("scalar", lambda s, oh=oh, ttile=ttile: s.activation(
                        out=mk.vt[:, ttile, oh * 512:(oh + 1) * 512], in_=mk.ps[2 + oh][:, :], func=AF.Identity),
                        wait=(mk.mm, mm_n), inc=(mk.sc, 1))
        fence(mk, "scalar")
        p.op("scalar", lambda s: s.activation(out=sq[0:64, 0:T], in_=lat[0:64, 0:T], func=AF.Square),
             inc=(mk.sc, 1))
        p.op("vector", lambda v: v.memset(mk.srow[0:1, 0:1], 0.0), wait=(mk.vv, mk.vv.n), inc=(mk.vv, 1))
        for qc in range(mk.TB512):
            p.op("tensor", lambda t, qc=qc: t.matmul(mk.ps[1][0:1, :], mk.ones32[0:64, 0:1],
                                                     sq[0:64, qc * 512:(qc + 1) * 512],
                                                     start=True, stop=True),
                 wait=[(mk.sc, mk.sc.n), (mk.vv, mk.vv.n)], inc=(mk.mm, 1))
            p.op("vector", lambda v: v.reduce_max(mk.srow[0:1, 1:2], mk.ps[1][0:1, :], axis=AX.X),
                 wait=(mk.mm, mk.mm.n), inc=(mk.vv, 1))
            p.op("vector", lambda v: v.tensor_tensor(mk.srow[0:1, 0:1], mk.srow[0:1, 0:1],
                                                     mk.srow[0:1, 1:2], ALU.max),
                 wait=(mk.vv, mk.vv.n), inc=(mk.vv, 1))
        p.op("vector", lambda v: v.tensor_scalar(mk.srow[0:1, 2:3], mk.srow[0:1, 0:1], -0.125, None, ALU.mult),
             wait=(mk.vv, mk.vv.n), inc=(mk.vv, 1))
        bcast_col(mk, mk.srow[0:1, 2:3], mk.nbias[:, :], mk.ps[1])
        fence(mk, "tensor")
        for qc in range(mk.TB512):
            nkt = (qc + 1) * 4
            for kt in range(nkt):
                bank = mk.ps[kt % 2]
                rows = slice(0, 64) if kt % 2 == 0 else slice(64, 128)
                pos = (0, 0) if kt % 2 == 0 else (64, 0)
                p.op("tensor", lambda t, kt=kt, qc=qc, bank=bank, rows=rows, pos=pos: t.matmul(
                    bank[:, :], lat[rows, kt * 128:(kt + 1) * 128], lat[rows, qc * 512:(qc + 1) * 512],
                    start=True, stop=True, tile_position=pos),
                    wait=[(mk.sc, mk.sc.n), (mk.gp, mk.gp.n)], inc=(mk.mm, 1))
                p.op("scalar", lambda s, kt=kt, bank=bank: s.activation(out=strip[:, soff + kt, :], in_=bank[:, :],
                                                                        func=AF.Exp, bias=mk.nbias[:, :], scale=0.125),
                     wait=[(mk.mm, mk.mm.n), (mk.vv, mk.vv.n)], inc=(mk.sc, 1))
                if kt >= qc * 4:
                    p.op("gpsimd", lambda g, kt=kt, qc=qc: g.affine_select(
                        strip[:, soff + kt, :], strip[:, soff + kt, :], pattern=[[1, 512]], compare_op=ALU.is_ge,
                        fill=0.0, base=qc * 512 - kt * 128, channel_multiplier=-1),
                        wait=(mk.sc, mk.sc.n), inc=(mk.gp, 1))
            gp_mask = mk.gp.n
            for kt in range(nkt):
                p.op("tensor", lambda t, kt=kt, nkt=nkt: t.matmul(mk.ps[6][0:1, :], mk.ones_bf[:, 0:1],
                                                                  strip[:, soff + kt, :],
                                                                  start=(kt == 0), stop=(kt == nkt - 1)),
                     wait=[(mk.gp, gp_mask), (mk.vv, mk.vv.n), (mk.sc, mk.sc.n)] if kt == 0 else None,
                     inc=(mk.mm, 1) if kt == nkt - 1 else None)
            p.op("vector", lambda v: v.tensor_scalar(mk.zrow[0:1, 1, :], mk.ps[6][0:1, :], 1e-12, None, ALU.add),
                 wait=(mk.mm, mk.mm.n), inc=(mk.vv, 1))
            p.op("vector", lambda v: v.reciprocal(mk.zrow[0:1, 0, :], mk.zrow[0:1, 1, :]),
                 wait=(mk.vv, mk.vv.n), inc=(mk.vv, 1))
            p.op("tensor", lambda t: t.matmul(mk.ps[7][:, :], mk.ones32[0:1, 0:128], mk.zrow[0:1, 0, :],
                                              start=True, stop=True), wait=(mk.vv, mk.vv.n), inc=(mk.mm, 1))
            for og in range(2):
                for oc4 in range(4):
                    oc = og * 4 + oc4
                    bank = mk.ps[2 + oc4]
                    for kt in range(nkt):
                        p.op("tensor", lambda t, kt=kt, oc=oc, bank=bank, nkt=nkt: t.matmul(
                            bank[:, :], mk.vt[:, kt, oc * 128:(oc + 1) * 128], strip[:, soff + kt, :],
                            start=(kt == 0), stop=(kt == nkt - 1)),
                            wait=(mk.sc, mk.sc.n) if kt == 0 else None,
                            inc=(mk.mm, 1) if kt == nkt - 1 else None)
                mm_av = mk.mm.n
                for oc4 in range(4):
                    oc = og * 4 + oc4
                    p.op("scalar", lambda s, oc4=oc4: s.activation(
                        out=mk.ysc[:, oc4 % 2, :], in_=mk.ps[2 + oc4][:, :], func=AF.Identity),
                        wait=[(mk.mm, mm_av), (mk.vv, mk.vv.n)], inc=(mk.sc, 1))
                    p.op("vector", lambda v, oc=oc, qc=qc, oc4=oc4: v.tensor_tensor(
                        mk.qt[:, oc, qc * 512:(qc + 1) * 512], mk.ysc[:, oc4 % 2, :], mk.ps[7][:, :], ALU.mult),
                        wait=(mk.sc, mk.sc.n), inc=(mk.vv, 1))
        reload_w(mk, mk.wB, woT)
        fence(mk, "tensor")
        fence(mk, "sync")
        outproj_residual(mk, mk.qt, mk.wB, borow, xD, xoutD, seq, xlnoutD=xlnoutD)
        if seq + 1 < mk.B:
            reload_w(mk, mk.wB, wvT)
    fence_all(mk)



def attn_core(mk, seq, kt_range, masks_for, ynT=None):
    p = mk.p
    ynT = ynT if ynT is not None else mk.qt
    p.wait("scalar", mk.lsto, mk.lsto.n)
    p.wait("gpsimd", mk.lsto, mk.lsto.n)
    psA = [mk.ps[0], mk.ps[1]]
    psB = [mk.ps[2], mk.ps[3]]
    psY = mk.ps[4]
    psZ = mk.ps[5]
    psBC = mk.ps[6]
    for hp in range(8):
        for qc in range(mk.TB512):
            lo, hi = kt_range(qc)
            qsA = mk.qt[0:64, hp, qc * 512:(qc + 1) * 512]
            qsB = mk.qt[64:128, hp, qc * 512:(qc + 1) * 512]
            cons = {0: (0, 0), 1: (0, 0), 2: (0, 0), 3: (0, 0)}
            for kt in range(lo, hi):
                sl = (kt - lo) % 2
                ptA = mk.pt[:, sl, :]
                ptB = mk.pt[:, 2 + sl, :]
                p.op("tensor", lambda t, kt=kt, hp=hp, sl=sl, qsA=qsA: t.matmul(
                    psA[sl][:, :], mk.kt[0:64, hp, kt * 128:(kt + 1) * 128], qsA,
                    start=True, stop=True, tile_position=(0, 0)),
                    wait=(mk.sc, mk.sc.n), inc=(mk.mm, 1))
                p.op("tensor", lambda t, kt=kt, hp=hp, sl=sl, qsB=qsB: t.matmul(
                    psB[sl][:, :], mk.kt[64:128, hp, kt * 128:(kt + 1) * 128], qsB,
                    start=True, stop=True, tile_position=(64, 0)), inc=(mk.mm, 1))
                p.op("scalar", lambda s, sl=sl, ptA=ptA: s.activation(
                    out=ptA, in_=psA[sl][:, :], func=AF.Exp, bias=mk.nbias[:, :], scale=0.125),
                    wait=[(mk.mm, mk.mm.n), (mk.gp, mk.gp.n)], inc=(mk.sc, 1))
                p.op("scalar", lambda s, sl=sl, ptB=ptB: s.activation(
                    out=ptB, in_=psB[sl][:, :], func=AF.Exp, bias=mk.nbias[:, :], scale=0.125),
                    inc=(mk.sc, 1))
                for (base, cm, step, cmp) in masks_for(kt, qc):
                    p.op("gpsimd", lambda g, base=base, cm=cm, step=step, cmp=cmp, ptA=ptA: g.affine_select(
                        ptA, ptA, pattern=[[step, 512]], compare_op=cmp, fill=0.0,
                        base=base, channel_multiplier=cm),
                        wait=[(mk.sc, mk.sc.n), (mk.gp, mk.gp.n)], inc=(mk.gp, 1))
                    p.op("gpsimd", lambda g, base=base, cm=cm, step=step, cmp=cmp, ptB=ptB: g.affine_select(
                        ptB, ptB, pattern=[[step, 512]], compare_op=cmp, fill=0.0,
                        base=base, channel_multiplier=cm), wait=(mk.gp, mk.gp.n), inc=(mk.gp, 1))
                gp_m = mk.gp.n
                first, last = (kt == lo), (kt == hi - 1)
                p.op("tensor", lambda t, first=first, last=last, ptA=ptA: t.matmul(
                    psZ[0:1, :], mk.ones_bf[:, 0:1], ptA, start=first, stop=last,
                    tile_position=(0, 0), skip_group_check=True),
                    wait=[(mk.gp, gp_m), (mk.sc, mk.sc.n), (mk.vv, mk.vv.n)], inc=(mk.mm, 1))
                p.op("tensor", lambda t, first=first, last=last, ptB=ptB: t.matmul(
                    psZ[64:65, :], mk.ones_bf[:, 0:1], ptB, start=first, stop=last,
                    tile_position=(0, 64), skip_group_check=True), inc=(mk.mm, 1))
                p.op("tensor", lambda t, kt=kt, hp=hp, first=first, last=last, ptA=ptA: t.matmul(
                    psY[0:64, :], mk.vt[:, kt, hp * 128:hp * 128 + 64], ptA,
                    start=first, stop=last, tile_position=(0, 0), skip_group_check=True), inc=(mk.mm, 1))
                p.op("tensor", lambda t, kt=kt, hp=hp, first=first, last=last, ptB=ptB: t.matmul(
                    psY[64:128, :], mk.vt[:, kt, hp * 128 + 64:hp * 128 + 128], ptB,
                    start=first, stop=last, tile_position=(0, 64), skip_group_check=True), inc=(mk.mm, 1))
            mm_av = mk.mm.n
            p.op("vector", lambda v: v.tensor_scalar(mk.zrow[0:1, 1, :], psZ[0:1, :], 1e-12, None, ALU.add),
                 wait=(mk.mm, mm_av), inc=(mk.vv, 1))
            p.op("vector", lambda v: v.tensor_scalar(mk.zrow[64:65, 1, :], psZ[64:65, :], 1e-12, None, ALU.add),
                 inc=(mk.vv, 1))
            p.op("vector", lambda v: v.reciprocal(mk.zrow[0:1, 0, :], mk.zrow[0:1, 1, :]),
                 wait=(mk.vv, mk.vv.n), inc=(mk.vv, 1))
            p.op("vector", lambda v: v.reciprocal(mk.zrow[64:65, 0, :], mk.zrow[64:65, 1, :]),
                 inc=(mk.vv, 1))
            vv_r = mk.vv.n
            p.op("tensor", lambda t: t.matmul(psBC[0:64, :], mk.ones32[0:1, 0:64], mk.zrow[0:1, 0, :],
                                              start=True, stop=True), wait=(mk.vv, vv_r), inc=(mk.mm, 1))
            p.op("tensor", lambda t: t.matmul(psBC[64:128, :], mk.ones32[64:65, 0:64], mk.zrow[64:65, 0, :],
                                              start=True, stop=True, tile_position=(64, 64)), inc=(mk.mm, 1))
            mm_bc = mk.mm.n
            p.op("scalar", lambda s: s.activation(out=mk.ysc[:, 0, :], in_=psY[:, :], func=AF.Identity),
                 wait=[(mk.mm, mm_av), (mk.vv, mk.vv.n)], inc=(mk.sc, 1))
            p.op("vector", lambda v, hp=hp, qc=qc: v.tensor_tensor(
                ynT[:, hp, qc * 512:(qc + 1) * 512], mk.ysc[:, 0, :], psBC[:, :], ALU.mult),
                wait=[(mk.sc, mk.sc.n), (mk.mm, mm_bc)], inc=(mk.vv, 1))


def bound_from_sq_fm(mk, buf, nch, pairnorm_ps, out_idx):
    p = mk.p
    T = mk.T
    p.op("vector", lambda v: v.memset(mk.srow[0:1, out_idx:out_idx + 1], 0.0),
         wait=(mk.vv, mk.vv.n), inc=(mk.vv, 1))
    for c in range(nch):
        p.op("scalar", lambda s, c=c: s.activation(out=mk.sq2[:, 0:T], in_=buf[:, c, 0:T], func=AF.Square),
             wait=[(mk.sc, mk.sc.n), (mk.vv, mk.vv.n), (mk.mm, mk.mm.n)], inc=(mk.sc, 1))
        for qp in range(T // 512):
            p.op("tensor", lambda t, qp=qp: t.matmul(pairnorm_ps[0:1, :], mk.ones32[:, 0:1],
                                                     mk.sq2[:, qp * 512:(qp + 1) * 512],
                                                     start=True, stop=True),
                 wait=[(mk.sc, mk.sc.n), (mk.vv, mk.vv.n)], inc=(mk.mm, 1))
            p.op("vector", lambda v: v.reduce_max(mk.srow[0:1, 8:9], pairnorm_ps[0:1, :], axis=AX.X),
                 wait=(mk.mm, mk.mm.n), inc=(mk.vv, 1))
            p.op("vector", lambda v, out_idx=out_idx: v.tensor_tensor(
                mk.srow[0:1, out_idx:out_idx + 1], mk.srow[0:1, out_idx:out_idx + 1],
                mk.srow[0:1, 8:9], ALU.max), wait=(mk.vv, mk.vv.n), inc=(mk.vv, 1))


def set_nbias_from_prod(mk, ia, ib, psbank):
    p = mk.p
    p.op("vector", lambda v: v.tensor_tensor(mk.srow[0:1, 9:10], mk.srow[0:1, ia:ia + 1],
                                             mk.srow[0:1, ib:ib + 1], ALU.mult),
         wait=(mk.vv, mk.vv.n), inc=(mk.vv, 1))
    p.op("scalar", lambda s: s.activation(out=mk.srow[0:1, 10:11], in_=mk.srow[0:1, 9:10], func=AF.Sqrt),
         wait=(mk.vv, mk.vv.n), inc=(mk.sc, 1))
    p.op("vector", lambda v: v.tensor_scalar(mk.srow[0:1, 11:12], mk.srow[0:1, 10:11], -0.125, None, ALU.mult),
         wait=(mk.sc, mk.sc.n), inc=(mk.vv, 1))
    bcast_col(mk, mk.srow[0:1, 11:12], mk.nbias[:, :], psbank)


def sub_A0b(mk, xD, xlnD, wqT, bqD, wkT, bkD, wvT, bvD, woT, boD, xoutD,
            skip_ln=False, xlnoutD=None):
    p = mk.p
    mk.phase_attn()
    mk.load_wT(mk.wA, wqT, 8192)
    mk.load_wT(mk.wB, wkT, 8192)
    bvrow = load_bias(mk, 0, bvD)
    borow = load_bias(mk, 1, boD)
    load_bias_row32(mk, 0, bqD)
    bias_cols_compute(mk, 0, mk.bcols[:, 0:8], mk.ps[7])
    load_bias_row32(mk, 1, bkD)
    bias_cols_compute(mk, 1, mk.bcols[:, 8:16], mk.ps[7])
    fence_all(mk)
    if not skip_ln:
        ln_stream(mk, xD, xlnD)

    def kt_range(qc):
        return (max(0, qc * 4 - 2), qc * 4 + 4)

    def masks_for(kt, qc):
        out = []
        if kt * 128 + 127 > qc * 512:
            out.append((qc * 512 - kt * 128, -1, 1, ALU.is_ge))
        if 255 - qc * 512 - 511 + kt * 128 + 127 < 0:
            out.append((255 - qc * 512 + kt * 128, 1, -1, ALU.is_ge))
        return out

    for seq in range(mk.B):
        fence_all(mk)
        qk_fm_seq(mk, xlnD, seq, bqcol=mk.bcols[:, 0:8], bkcol=mk.bcols[:, 8:16])
        reload_w(mk, mk.wA, wvT)
        v_tm_seq(mk, xlnD, seq, mk.wA, bvrow=bvrow)
        fence(mk, "scalar")
        bound_from_sq_fm(mk, mk.qt, 8, mk.ps[7], 0)
        bound_from_sq_fm(mk, mk.kt, 8, mk.ps[7], 1)
        set_nbias_from_prod(mk, 0, 1, mk.ps[7])
        fence(mk, "tensor")
        attn_core(mk, seq, kt_range, masks_for)
        reload_w(mk, mk.wB, woT)
        fence(mk, "tensor")
        fence(mk, "sync")
        outproj_residual(mk, mk.qt, mk.wB, borow, xD, xoutD, seq, xlnoutD=xlnoutD)
        if seq + 1 < mk.B:
            reload_w(mk, mk.wA, wqT)
            reload_w(mk, mk.wB, wkT)
    fence_all(mk)


def sub_A1a(mk, xD, xlnD, wkvdD, wkvuD, wqD, woD, xoutD, skip_ln=False, xlnoutD=None):
    p = mk.p
    mk.phase_attn()
    T = mk.T
    mk.load_wT(mk.wA, wkvdD, 1024)
    p.op("gpsimd", lambda g: g.dma_start(out=mk.wB[:, 0:2, :], in_=wkvuD.rearrange("p (c o) -> p c o", o=1024)),
         inc=(mk.ld, 16))
    p.op("gpsimd", lambda g: g.memset(mk.biasrows[0:1, 3, :], 0.0), inc=(mk.gp, 1))
    zbias = mk.biasrows[0:1, 3, :]
    kvup = mk.wB.rearrange("p c o -> p (c o)")
    fence_all(mk)
    if not skip_ln:
        ln_stream(mk, xD, xlnD)

    def kt_range(qc):
        return (0, qc * 4 + 4)

    def masks_for(kt, qc):
        if kt * 128 + 127 > qc * 512:
            return [(qc * 512 - kt * 128, -1, 1, ALU.is_ge)]
        return []

    for seq in range(mk.B):
        fence_all(mk)
        if seq > 0:
            reload_w(mk, mk.wA, wkvdD, nelem=1024)
        cur = load_xlt(mk, xlnD, seq, 0, 0)
        wkf = mk.wA.rearrange("p c t -> p (c t)")
        for blk in range(mk.TB512):
            xl, sem, ld_n = cur
            if blk + 1 < mk.TB512:
                cur = load_xlt(mk, xlnD, seq, blk + 1, blk + 1)
            for ci in range(8):
                p.op("tensor", lambda t, ci=ci, xl=xl, wkf=wkf: t.matmul(
                    mk.ps[0][:, :], wkf[:, ci * 128:(ci + 1) * 128], xl[:, ci, :],
                    start=(ci == 0), stop=(ci == 7)),
                    wait=[(sem, ld_n), (mk.sc, mk.sc.n)] if ci == 0 else None,
                    inc=(mk.mm, 1) if ci == 7 else None)
            p.op("scalar", lambda s, blk=blk: s.activation(out=mk.latkv[:, blk * 512:(blk + 1) * 512],
                                                           in_=mk.ps[0][:, :], func=AF.Identity),
                 wait=(mk.mm, mk.mm.n), inc=(mk.sc, 1))
        reload_w(mk, mk.wA, wqD)
        cur = load_xlt(mk, xlnD, seq, 0, 0)
        for blk in range(mk.TB512):
            xl, sem, ld_n = cur
            if blk + 1 < mk.TB512:
                cur = load_xlt(mk, xlnD, seq, blk + 1, blk + 1)
            for oc in range(8):
                bank = mk.ps[1 + oc % 2]
                for ci in range(8):
                    p.op("tensor", lambda t, ci=ci, oc=oc, bank=bank, xl=xl: t.matmul(
                        bank[:, :], mk.wA[:, ci, oc * 128:(oc + 1) * 128], xl[:, ci, :],
                        start=(ci == 0), stop=(ci == 7)),
                        wait=[(sem, ld_n), (mk.sc, mk.sc.n)] if ci == 0 else (
                            (mk.sc, mk.sc.n) if ci == 0 else None),
                        inc=(mk.mm, 1) if ci == 7 else None)
                p.op("scalar", lambda s, oc=oc, blk=blk, bank=bank: s.activation(
                    out=mk.qt[:, oc, blk * 512:(blk + 1) * 512], in_=bank[:, :], func=AF.Identity),
                    wait=(mk.mm, mk.mm.n), inc=(mk.sc, 1))
        fence(mk, "tensor")
        for blk in range(mk.TB512):
            for oc in range(8):
                bank = mk.ps[oc % 2]
                p.op("tensor", lambda t, oc=oc, blk=blk, bank=bank: t.matmul(
                    bank[:, :], kvup[:, oc * 128:(oc + 1) * 128], mk.latkv[:, blk * 512:(blk + 1) * 512],
                    start=True, stop=True), wait=(mk.sc, mk.sc.n), inc=(mk.mm, 1))
                p.op("scalar", lambda s, oc=oc, blk=blk, bank=bank: s.activation(
                    out=mk.kt[:, oc, blk * 512:(blk + 1) * 512], in_=bank[:, :], func=AF.Identity),
                    wait=(mk.mm, mk.mm.n), inc=(mk.sc, 1))
        for tt in range(mk.TT):
            for oh in range(2):
                bank = mk.ps[2 + oh]
                p.op("tensor", lambda t, tt=tt, oh=oh, bank=bank: t.matmul(
                    bank[:, :], mk.latkv[:, tt * 128:(tt + 1) * 128],
                    kvup[:, 1024 + oh * 512:1024 + (oh + 1) * 512],
                    start=True, stop=True), wait=(mk.sc, mk.sc.n), inc=(mk.mm, 1))
            mm_n = mk.mm.n
            for oh in range(2):
                p.op("scalar", lambda s, tt=tt, oh=oh: s.activation(
                    out=mk.vt[:, tt, oh * 512:(oh + 1) * 512], in_=mk.ps[2 + oh][:, :], func=AF.Identity),
                    wait=(mk.mm, mm_n), inc=(mk.sc, 1))
        fence(mk, "scalar")
        bound_from_sq_fm(mk, mk.qt, 8, mk.ps[7], 0)
        bound_from_sq_fm(mk, mk.kt, 8, mk.ps[7], 1)
        set_nbias_from_prod(mk, 0, 1, mk.ps[7])
        fence(mk, "tensor")
        attn_core(mk, seq, kt_range, masks_for)
        reload_w(mk, mk.wB, woD)
        fence(mk, "tensor")
        fence(mk, "sync")
        outproj_residual(mk, mk.qt, mk.wB, zbias, xD, xoutD, seq, xlnoutD=xlnoutD)
        if seq + 1 < mk.B:
            p.op("gpsimd", lambda g: g.dma_start(out=mk.wB[:, 0:2, :],
                                                 in_=wkvuD.rearrange("p (c o) -> p c o", o=1024)),
                 wait=[(mk.mm, mk.mm.n), (mk.sc, mk.sc.n), (mk.vv, mk.vv.n)], inc=(mk.ld, 16))
            fence(mk, "tensor")
    fence_all(mk)


def build_rope_tables(mk):
    p = mk.p
    p.op("gpsimd", lambda g: g.iota(mk.srow[0:1, 16:48], pattern=[[1, 32]], base=0,
                                    channel_multiplier=0, allow_small_or_imprecise_dtypes=True),
         wait=(mk.gp, mk.gp.n), inc=(mk.gp, 1))
    p.op("vector", lambda v: v.tensor_scalar(mk.srow[0:1, 16:48], mk.srow[0:1, 16:48], 1e-4, None, ALU.mult),
         wait=[(mk.gp, mk.gp.n), (mk.vv, mk.vv.n)], inc=(mk.vv, 1))
    for tl in range(mk.TT):
        p.op("gpsimd", lambda g, tl=tl: g.iota(mk.srow[0:1, 48:64].rearrange("p f -> p f") if False else mk.zrow[0:1, 1, 0:128],
                                               pattern=[[1, 128]], base=tl * 128,
                                               channel_multiplier=0, allow_small_or_imprecise_dtypes=True),
             wait=[(mk.gp, mk.gp.n), (mk.mm, mk.mm.n)], inc=(mk.gp, 1))
        p.op("tensor", lambda t: t.matmul(mk.ps[7][:, 0:32], mk.zrow[0:1, 1, 0:128], mk.srow[0:1, 16:48],
                                          start=True, stop=True),
             wait=[(mk.gp, mk.gp.n), (mk.vv, mk.vv.n)], inc=(mk.mm, 1))
        p.op("vector", lambda v: v.tensor_scalar(mk.sq2[:, 0:32],
                                                 mk.ps[7][:, 0:32], 3.141592653589793, 6.283185307179586,
                                                 ALU.is_gt, ALU.mult),
             wait=[(mk.mm, mk.mm.n), (mk.sc, mk.sc.n), (mk.vv, mk.vv.n)], inc=(mk.vv, 1))
        p.op("vector", lambda v: v.tensor_tensor(mk.sq2[:, 32:64], mk.ps[7][:, 0:32], mk.sq2[:, 0:32],
                                                 ALU.subtract), wait=[(mk.vv, mk.vv.n), (mk.sc, mk.sc.n)],
             inc=(mk.vv, 1))
        p.op("scalar", lambda s, tl=tl: s.activation(out=mk.cossin[:, 1, tl, :], in_=mk.sq2[:, 32:64],
                                                     func=AF.Sin), wait=[(mk.vv, mk.vv.n), (mk.sc, mk.sc.n)],
             inc=(mk.sc, 1))
        p.op("scalar", lambda s: s.activation(out=mk.sq2[:, 64:96], in_=mk.sq2[:, 32:64], func=AF.Abs),
             wait=(mk.sc, mk.sc.n), inc=(mk.sc, 1))
        p.op("scalar", lambda s, tl=tl: s.activation(out=mk.cossin[:, 0, tl, :], in_=mk.sq2[:, 64:96],
                                                     func=AF.Sin, bias=mk.pihalf[:, :], scale=-1.0),
             wait=(mk.sc, mk.sc.n), inc=(mk.sc, 1))


def rope_tile(mk, tl, src_f32, dst_bf16):
    p = mk.p
    qr = src_f32.rearrange("p (h i two) -> p h i two", h=16, two=2)[:, :, :, 0]
    qi = src_f32.rearrange("p (h i two) -> p h i two", h=16, two=2)[:, :, :, 1]
    dr = dst_bf16.rearrange("p (h i two) -> p h i two", h=16, two=2)[:, :, :, 0]
    di = dst_bf16.rearrange("p (h i two) -> p h i two", h=16, two=2)[:, :, :, 1]
    cos = mk.cossin[:, 0, tl:tl + 1, :].broadcast_to([128, 16, 32])
    sin = mk.cossin[:, 1, tl:tl + 1, :].broadcast_to([128, 16, 32])
    t0 = mk.sq2[:, 1024:1536].rearrange("p (h i) -> p h i", h=16)
    t1 = mk.sq2[:, 1536:2048].rearrange("p (h i) -> p h i", h=16)
    sc0 = t0[:, 0:16, 0:32]
    sc1 = t1[:, 0:16, 0:32]
    p.op("vector", lambda v: v.tensor_tensor(sc0, qr, cos, ALU.mult),
         wait=[(mk.vv, mk.vv.n), (mk.lsto, mk.lsto.n)], inc=(mk.vv, 1))
    p.op("vector", lambda v: v.tensor_tensor(sc1, qi, sin, ALU.mult), wait=(mk.vv, mk.vv.n), inc=(mk.vv, 1))
    p.op("vector", lambda v: v.tensor_tensor(dr, sc0, sc1, ALU.subtract), wait=(mk.vv, mk.vv.n), inc=(mk.vv, 1))
    p.op("vector", lambda v: v.tensor_tensor(sc0, qr, sin, ALU.mult), wait=(mk.vv, mk.vv.n), inc=(mk.vv, 1))
    p.op("vector", lambda v: v.tensor_tensor(sc1, qi, cos, ALU.mult), wait=(mk.vv, mk.vv.n), inc=(mk.vv, 1))
    p.op("vector", lambda v: v.tensor_tensor(di, sc0, sc1, ALU.add), wait=(mk.vv, mk.vv.n), inc=(mk.vv, 1))


def sub_A1b(mk, xD, xlnD, wqT, bqD, wkT, bkD, wvT, bvD, woT, boD, qrotD, krotD, xoutD,
            skip_ln=False, xlnoutD=None):
    p = mk.p
    mk.phase_attn()
    T = mk.T
    mk.load_wT(mk.wA, wqT, 8192)
    mk.load_wT(mk.wB, wkT, 8192)
    bqrow = load_bias(mk, 0, bqD)
    bkrow = load_bias(mk, 1, bkD)
    bvrow = load_bias(mk, 2, bvD)
    borow = load_bias(mk, 3, boD)
    fence_all(mk)
    if not getattr(mk, "_rope_built", False):
        build_rope_tables(mk)
        mk._rope_built = True
        fence_all(mk)
    if not skip_ln:
        ln_stream(mk, xD, xlnD)

    def kt_range(qc):
        return (0, mk.TT)

    def masks_for(kt, qc):
        return []

    for seq in range(mk.B):
        fence_all(mk)
        p.op("vector", lambda v: v.memset(mk.cols[:, 8:16], 0.0), wait=(mk.vv, mk.vv.n), inc=(mk.vv, 1))
        p.op("vector", lambda v: v.memset(mk.srow[:, 32:40], 0.0), wait=(mk.vv, mk.vv.n), inc=(mk.vv, 1))
        cur = load_xlt(mk, xlnD, seq, 0, 0)
        for blk in range(mk.TB512):
            xl, sem, ld_n = cur
            if blk + 1 < mk.TB512:
                cur = load_xlt(mk, xlnD, seq, blk + 1, blk + 1)
            for tt4 in range(4):
                tl = blk * 4 + tt4
                for (w, brow, dstD, ncol) in ((mk.wA, bqrow, qrotD, mk.cols[:, 8:16]),
                                              (mk.wB, bkrow, krotD, mk.srow[:, 32:40])):
                    for oh in range(2):
                        bank = mk.ps[oh]
                        for ci in range(8):
                            p.op("tensor", lambda t, ci=ci, oh=oh, tt4=tt4, w=w, bank=bank, xl=xl: t.matmul(
                                bank[:, :], xl[:, ci, tt4 * 128:(tt4 + 1) * 128],
                                w[:, ci, oh * 512:(oh + 1) * 512], start=(ci == 0), stop=False),
                                wait=[(sem, ld_n), (mk.sc, mk.sc.n)] if ci == 0 else None)
                        p.op("tensor", lambda t, oh=oh, brow=brow, bank=bank: t.matmul(
                            bank[:, :], mk.ones_bf[0:1, :], brow[:, oh * 512:(oh + 1) * 512],
                            start=False, stop=True), inc=(mk.mm, 1))
                    mm_n = mk.mm.n
                    for oh in range(2):
                        p.op("scalar", lambda s, oh=oh: s.activation(
                            out=mk.sq2[:, oh * 512:(oh + 1) * 512], in_=mk.ps[oh][:, :], func=AF.Identity),
                            wait=[(mk.mm, mm_n), (mk.vv, mk.vv.n)], inc=(mk.sc, 1))
                    p.op("scalar", lambda s: s.activation(out=mk.xt[:, 1, :], in_=mk.sq2[:, 0:1024],
                                                          func=AF.Square), wait=(mk.sc, mk.sc.n), inc=(mk.sc, 1))
                    p.op("vector", lambda v: v.tensor_reduce(
                        mk.cols[:, 0:8], mk.xt[:, 1, :].rearrange("p (c f) -> p c f", c=8), AX.X, ALU.add),
                        wait=[(mk.sc, mk.sc.n), (mk.vv, mk.vv.n)], inc=(mk.vv, 1))
                    p.op("vector", lambda v, ncol=ncol: v.tensor_tensor(ncol, ncol, mk.cols[:, 0:8], ALU.max),
                         wait=(mk.vv, mk.vv.n), inc=(mk.vv, 1))
                    rope_tile(mk, tl, mk.sq2[:, 0:1024], mk.xln[:, tt4 % 2, :])
                    p.op("sync", lambda s, tl=tl, dstD=dstD, sl=tt4 % 2: s.dma_start(
                        out=dstD[tl * 128:(tl + 1) * 128, :], in_=mk.xln[:, sl, :]),
                        wait=(mk.vv, mk.vv.n), inc=(mk.lsto, 16))
        reload_w(mk, mk.wA, wvT)
        v_tm_seq(mk, xlnD, seq, mk.wA, bvrow=bvrow)
        for (ncol, idx) in ((mk.cols[:, 8:16], 0), (mk.srow[:, 32:40], 1)):
            p.op("vector", lambda v, ncol=ncol: v.reduce_max(mk.cols[:, 0:1], ncol, axis=AX.X),
                 wait=(mk.vv, mk.vv.n), inc=(mk.vv, 1))
            p.op("tensor", lambda t: t.matmul(mk.ps[7][0:1, 0:128], mk.cols[:, 0:1], mk.ident32[:, 0:128],
                                              start=True, stop=True),
                 wait=(mk.vv, mk.vv.n), inc=(mk.mm, 1))
            p.op("vector", lambda v, idx=idx: v.reduce_max(mk.srow[0:1, idx:idx + 1], mk.ps[7][0:1, 0:128], axis=AX.X),
                 wait=(mk.mm, mk.mm.n), inc=(mk.vv, 1))
        set_nbias_from_prod(mk, 0, 1, mk.ps[7])
        fence(mk, "tensor")
        for (dstbuf, srcD) in ((mk.qt, qrotD), (mk.kt, krotD)):
            for c in range(8):
                p.op("sync", lambda s, c=c, dstbuf=dstbuf, srcD=srcD, seq=seq: s.dma_start(
                    out=dstbuf[:, c, :], in_=srcD[0:T, c * 128:(c + 1) * 128], transpose=True),
                    wait=[(mk.lsto, mk.lsto.n), (mk.mm, mk.mm.n)] if c == 0 else None,
                    inc=(mk.lst, 16))
        p.wait("tensor", mk.lst, mk.lst.n)
        attn_core(mk, seq, kt_range, masks_for)
        reload_w(mk, mk.wB, woT)
        fence(mk, "tensor")
        fence(mk, "sync")
        outproj_residual(mk, mk.qt, mk.wB, borow, xD, xoutD, seq, xlnoutD=xlnoutD)
        if seq + 1 < mk.B:
            reload_w(mk, mk.wA, wqT)
            reload_w(mk, mk.wB, wkT)
    fence_all(mk)



def f0_routing(mk, xlnD, gwT_D, lbD, selbuf, statsD):
    p = mk.p
    if not hasattr(mk, "rtw"):
        mk.rtw = p.sb("b_rtw", [128, 8, 8], BF16)
    mk.load_wT(mk.rtw, gwT_D, 64, wait=[(mk.mm, mk.mm.n), (mk.sc, mk.sc.n)])
    lbrow = load_bias(mk, 0, lbD)
    fence_all(mk)
    c = mk.cols
    acc = mk.sq2[:, 1024:1040]
    xs = mk.sq2[:, 1040:1056]
    p.op("vector", lambda v: v.memset(acc[:, 0:16], 0.0), inc=(mk.vv, 1))
    cur = load_xlt(mk, xlnD, 0, 0, 0)
    for gb in range(mk.NB512):
        xl, sem, ld_n = cur
        if gb + 1 < mk.NB512:
            seq1, blk1 = divmod(gb + 1, mk.TB512)
            cur = load_xlt(mk, xlnD, seq1, blk1, gb + 1)
        for t4 in range(4):
            gtile = gb * 4 + t4
            for ci in range(8):
                p.op("tensor", lambda t, ci=ci, t4=t4, xl=xl: t.matmul(
                    mk.ps[0][:, 0:8], xl[:, ci, t4 * 128:(t4 + 1) * 128], mk.rtw[:, ci, 0:8],
                    start=(ci == 0), stop=False),
                    wait=[(sem, ld_n), (mk.sc, mk.sc.n)] if ci == 0 else None)
            p.op("tensor", lambda t: t.matmul(mk.ps[0][:, 0:8], mk.ones_bf[0:1, :], lbrow[:, 0:8],
                                              start=False, stop=True), inc=(mk.mm, 1))
            p.op("scalar", lambda s: s.activation(out=c[:, 0:8], in_=mk.ps[0][:, 0:8], func=AF.Sigmoid),
                 wait=[(mk.mm, mk.mm.n), (mk.vv, mk.vv.n)], inc=(mk.sc, 1))
            p.op("vector", lambda v: v.reduce_max(c[:, 8:9], c[:, 0:8], axis=AX.X),
                 wait=(mk.sc, mk.sc.n), inc=(mk.vv, 1))
            p.op("vector", lambda v: v.tensor_scalar(mk.xsq[:, 0:8], c[:, 0:8], c[:, 8:9], None, ALU.is_equal),
                 wait=(mk.vv, mk.vv.n), inc=(mk.vv, 1))
            p.op("vector", lambda v: v.scalar_tensor_tensor(c[:, 0:8], mk.xsq[:, 0:8], -1e9, c[:, 0:8],
                                                            ALU.mult, ALU.add),
                 wait=(mk.vv, mk.vv.n), inc=(mk.vv, 1))
            p.op("vector", lambda v: v.reduce_max(c[:, 9:10], c[:, 0:8], axis=AX.X),
                 wait=(mk.vv, mk.vv.n), inc=(mk.vv, 1))
            p.op("vector", lambda v: v.tensor_tensor(c[:, 10:11], c[:, 8:9], c[:, 9:10], ALU.add),
                 wait=(mk.vv, mk.vv.n), inc=(mk.vv, 1))
            p.op("vector", lambda v: v.tensor_scalar(c[:, 10:11], c[:, 10:11], 1e-8, None, ALU.add),
                 wait=(mk.vv, mk.vv.n), inc=(mk.vv, 1))
            p.op("vector", lambda v: v.reciprocal(c[:, 11:12], c[:, 10:11]),
                 wait=(mk.vv, mk.vv.n), inc=(mk.vv, 1))
            p.op("vector", lambda v: v.tensor_tensor(c[:, 12:13], c[:, 8:9], c[:, 11:12], ALU.mult),
                 wait=(mk.vv, mk.vv.n), inc=(mk.vv, 1))
            p.op("vector", lambda v: v.tensor_tensor(c[:, 13:14], c[:, 9:10], c[:, 11:12], ALU.mult),
                 wait=(mk.vv, mk.vv.n), inc=(mk.vv, 1))
            p.op("vector", lambda v: v.tensor_scalar(xs[:, 0:8], c[:, 0:8], -1e8, None, ALU.is_lt),
                 wait=(mk.vv, mk.vv.n), inc=(mk.vv, 1))
            p.op("vector", lambda v: v.tensor_scalar(xs[:, 8:16], c[:, 0:8], c[:, 9:10], None, ALU.is_equal),
                 wait=(mk.vv, mk.vv.n), inc=(mk.vv, 1))
            p.op("vector", lambda v, gtile=gtile: v.tensor_tensor(selbuf[:, gtile, :], xs[:, 0:8],
                                                                  xs[:, 8:16], ALU.add),
                 wait=(mk.vv, mk.vv.n), inc=(mk.vv, 1))
            p.op("vector", lambda v, gtile=gtile: v.tensor_tensor(acc[:, 0:8], acc[:, 0:8],
                                                                  selbuf[:, gtile, :], ALU.add),
                 wait=(mk.vv, mk.vv.n), inc=(mk.vv, 1))
            p.op("vector", lambda v: v.tensor_scalar(xs[:, 0:8], xs[:, 0:8], c[:, 12:13], None, ALU.mult),
                 wait=(mk.vv, mk.vv.n), inc=(mk.vv, 1))
            p.op("vector", lambda v: v.tensor_scalar(xs[:, 8:16], xs[:, 8:16], c[:, 13:14], None, ALU.mult),
                 wait=(mk.vv, mk.vv.n), inc=(mk.vv, 1))
            p.op("vector", lambda v: v.tensor_tensor(xs[:, 0:8], xs[:, 0:8], xs[:, 8:16], ALU.add),
                 wait=(mk.vv, mk.vv.n), inc=(mk.vv, 1))
            p.op("vector", lambda v: v.tensor_tensor(acc[:, 8:16], acc[:, 8:16], xs[:, 0:8], ALU.add),
                 wait=(mk.vv, mk.vv.n), inc=(mk.vv, 1))
    p.op("tensor", lambda t: t.matmul(mk.ps[1][0:8, 0:128], acc[:, 0:8], mk.ident32[:, :],
                                      start=True, stop=True), wait=(mk.vv, mk.vv.n), inc=(mk.mm, 1))
    p.op("vector", lambda v: v.reduce_sum(c[0:8, 0:1], mk.ps[1][0:8, 0:128], axis=AX.X),
         wait=(mk.mm, mk.mm.n), inc=(mk.vv, 1))
    p.op("tensor", lambda t: t.matmul(mk.ps[1][0:8, 0:128], acc[:, 8:16], mk.ident32[:, :],
                                      start=True, stop=True), wait=(mk.vv, mk.vv.n), inc=(mk.mm, 1))
    p.op("vector", lambda v: v.reduce_sum(c[0:8, 1:2], mk.ps[1][0:8, 0:128], axis=AX.X),
         wait=(mk.mm, mk.mm.n), inc=(mk.vv, 1))
    p.op("sync", lambda s: s.dma_start(out=statsD, in_=c[0:8, 0:2]), wait=(mk.vv, mk.vv.n), inc=(mk.lsto, 16))


def f0_wmean_from_stats(mk, statsAllD, sumselD):
    p = mk.p
    c = mk.cols
    p.op("sync", lambda s: s.dma_start(out=mk.srow[0:64, 16:18], in_=statsAllD), inc=(mk.lsb, 16))
    p.op("sync", lambda s: s.dma_start(out=mk.srow[0:64, 18:26], in_=sumselD), inc=(mk.lsb, 16))
    p.op("tensor", lambda t: t.matmul(mk.ps[0][0:8, 0:2], mk.srow[0:64, 18:26], mk.srow[0:64, 16:18],
                                      start=True, stop=True),
         wait=[(mk.lsb, mk.lsb.n), (mk.vv, mk.vv.n), (mk.gp, mk.gp.n)], inc=(mk.mm, 1))
    p.op("vector", lambda v: v.tensor_copy(c[0:8, 0:2], mk.ps[0][0:8, 0:2]),
         wait=[(mk.mm, mk.mm.n), (mk.lsto, mk.lsto.n)], inc=(mk.vv, 1))
    p.op("vector", lambda v: v.tensor_scalar(c[0:8, 2:3], c[0:8, 0:1], 1.0, None, ALU.max),
         wait=(mk.vv, mk.vv.n), inc=(mk.vv, 1))
    p.op("vector", lambda v: v.reciprocal(c[0:8, 3:4], c[0:8, 2:3]), wait=(mk.vv, mk.vv.n), inc=(mk.vv, 1))
    p.op("vector", lambda v: v.tensor_tensor(c[0:8, 4:5], c[0:8, 1:2], c[0:8, 3:4], ALU.mult),
         wait=(mk.vv, mk.vv.n), inc=(mk.vv, 1))
    p.op("tensor", lambda t: t.matmul(mk.ps[0][0:1, 0:8], c[0:8, 4:5], mk.ident32[0:8, 0:8],
                                      start=True, stop=True), wait=(mk.vv, mk.vv.n), inc=(mk.mm, 1))
    p.op("vector", lambda v: v.tensor_copy(mk.srow[0:1, 26:34], mk.ps[0][0:1, 0:8]),
         wait=(mk.mm, mk.mm.n), inc=(mk.vv, 1))
    p.op("tensor", lambda t: t.matmul(mk.ps[0][:, 0:8], mk.ones32[0:1, 0:128], mk.srow[0:1, 26:34],
                                      start=True, stop=True), wait=(mk.vv, mk.vv.n), inc=(mk.mm, 1))
    wmean_bc = mk.sq2[:, 1056:1064]
    p.op("vector", lambda v: v.tensor_copy(wmean_bc, mk.ps[0][:, 0:8]), wait=(mk.mm, mk.mm.n), inc=(mk.vv, 1))
    return wmean_bc


def f0_experts(mk, xD, xlnD, selbuf, wmean_bc, expert_ws, xoutD,
               xlt2048, accbuf, hbuf, xlnoutD=None):
    p = mk.p
    guwD, edwD = expert_ws
    wAf = mk.wA.rearrange("p c t -> p (c t)")
    wBf = mk.wB.rearrange("p c t -> p (c t)")
    c = mk.cols
    SBS = min(2048, mk.N)
    NSB = mk.N // SBS
    NJ = SBS // 512
    for sb in range(NSB):
        fence(mk, "sync")
        for q4 in range(NJ):
            gb = sb * NJ + q4
            seq, blk = divmod(gb, mk.TB512)
            base = seq * mk.T + blk * 512
            for ch in range(8):
                p.op("sync", lambda s, ch=ch, base=base, q4=q4: s.dma_start(
                    out=xlt2048[:, ch, q4 * 512:(q4 + 1) * 512],
                    in_=xlnD[base:base + 512, ch * 128:(ch + 1) * 128], transpose=True),
                    inc=(mk.lst, 16))
        ld_n = mk.lst.n
        p.wait("tensor", mk.lst, ld_n)
        for e in range(10):
            shared = (e >= 8)
            ew_waits = [(mk.mm, mk.mm.n), (mk.sc, mk.sc.n)]
            p.op("gpsimd", lambda g, e=e: g.dma_start(
                out=wAf[:, 0:4096], in_=guwD[e * 128:(e + 1) * 128, 0:4096]),
                wait=list(ew_waits), inc=(mk.ld, 16))
            p.op("gpsimd", lambda g, e=e: g.dma_start(
                out=wAf[:, 4096:8192], in_=guwD[e * 128:(e + 1) * 128, 4096:8192]),
                wait=list(ew_waits), inc=(mk.ld, 16))
            p.op("gpsimd", lambda g, e=e: g.dma_start(
                out=wBf[:, 0:4096], in_=edwD[e * 128:(e + 1) * 128, :]),
                wait=(mk.mm, mk.mm.n), inc=(mk.ld, 16))
            ldw = mk.ld.n
            p.wait("tensor", mk.ld, ldw)
            for j in range(NJ):
                for fo in range(4):
                    for ci in range(8):
                        wg = mk.wA[:, ci, fo * 128:(fo + 1) * 128]
                        p.op("tensor", lambda t, wg=wg, j=j, ci=ci: t.matmul(
                            mk.ps[0][:, :], wg, xlt2048[:, ci, j * 512:(j + 1) * 512],
                            start=(ci == 0), stop=(ci == 7)),
                            wait=(mk.sc, mk.sc.n) if ci == 0 else None,
                            inc=(mk.mm, 1) if ci == 7 else None)
                    for ci in range(8):
                        wu = mk.wA[:, ci, 512 + fo * 128:512 + (fo + 1) * 128]
                        p.op("tensor", lambda t, wu=wu, j=j, ci=ci: t.matmul(
                            mk.ps[1][:, :], wu, xlt2048[:, ci, j * 512:(j + 1) * 512],
                            start=(ci == 0), stop=(ci == 7)),
                            wait=(mk.vv, mk.vv.n) if ci == 0 else None,
                            inc=(mk.mm, 1) if ci == 7 else None)
                    mm_n = mk.mm.n
                    p.op("scalar", lambda s, fo=fo: s.activation(out=hbuf[:, fo, :], in_=mk.ps[0][:, :],
                                                                 func=AF.Silu),
                         wait=[(mk.mm, mm_n), (mk.vv, mk.vv.n)], inc=(mk.sc, 1))
                    p.op("vector", lambda v, fo=fo: v.tensor_tensor(hbuf[:, fo, :], hbuf[:, fo, :],
                                                                    mk.ps[1][:, :], ALU.mult),
                         wait=(mk.sc, mk.sc.n), inc=(mk.vv, 1))
                vv_h = mk.vv.n
                for tt in range(4):
                    ltile = j * 4 + tt
                    gtile = sb * (SBS // 128) + ltile
                    for oh in range(2):
                        bank = mk.ps[2 + oh]
                        for fo in range(4):
                            p.op("tensor", lambda t, fo=fo, tt=tt, oh=oh, bank=bank: t.matmul(
                                bank[:, :], hbuf[:, fo, tt * 128:(tt + 1) * 128],
                                mk.wB[:, fo, oh * 512:(oh + 1) * 512],
                                start=(fo == 0), stop=(fo == 3)),
                                wait=[(mk.vv, vv_h if fo == 0 and oh == 0 and tt == 0 else mk.vv.n)] if fo == 0 else None,
                                inc=(mk.mm, 1) if fo == 3 else None)
                    mm_y = mk.mm.n
                    if not shared:
                        p.op("vector", lambda v, gtile=gtile, e=e: v.tensor_tensor(
                            c[:, 14:15], selbuf[:, gtile, e:e + 1], wmean_bc[:, e:e + 1], ALU.mult),
                            wait=[(mk.mm, mm_y), (mk.vv, mk.vv.n)], inc=(mk.vv, 1))
                    for oh in range(2):
                        p.op("scalar", lambda s, oh=oh: s.activation(
                            out=mk.xsq[:, oh * 512:(oh + 1) * 512], in_=mk.ps[2 + oh][:, :], func=AF.Identity),
                            wait=[(mk.mm, mm_y), (mk.vv, mk.vv.n)], inc=(mk.sc, 1))
                    sc_y = mk.sc.n
                    for oh in range(2):
                        sl = accbuf[:, ltile, oh * 512:(oh + 1) * 512]
                        ysl = mk.xsq[:, oh * 512:(oh + 1) * 512]
                        if e == 0:
                            p.op("vector", lambda v, sl=sl, ysl=ysl: v.tensor_scalar(
                                sl, ysl, c[:, 14:15], None, ALU.mult),
                                wait=[(mk.sc, sc_y), (mk.vv, mk.vv.n)], inc=(mk.vv, 1))
                        elif not shared:
                            p.op("vector", lambda v, sl=sl, ysl=ysl: v.scalar_tensor_tensor(
                                sl, ysl, c[:, 14:15], sl, ALU.mult, ALU.add),
                                wait=[(mk.sc, sc_y), (mk.vv, mk.vv.n)], inc=(mk.vv, 1))
                        else:
                            p.op("vector", lambda v, sl=sl, ysl=ysl: v.tensor_tensor(
                                sl, ysl, sl, ALU.add),
                                wait=[(mk.sc, sc_y), (mk.vv, mk.vv.n)], inc=(mk.vv, 1))
        for ltile in range(SBS // 128):
            gtile = sb * (SBS // 128) + ltile
            slp = ltile % 2
            p.op("sync", lambda s, g=gtile, slp=slp: s.dma_start(out=mk.xt[:, slp, :], in_=xD[g * 128:(g + 1) * 128, :]),
                 wait=[(mk.vv, mk.vv.n), (mk.sc, mk.sc.n)], inc=(mk.lsx[slp], 16))
            p.op("vector", lambda v, ltile=ltile, slp=slp: v.tensor_tensor(
                mk.xo[:, slp, :], accbuf[:, ltile, :], mk.xt[:, slp, :], ALU.add),
                wait=[(mk.lsx[slp], mk.lsx[slp].n), (mk.lsto, mk.lsto.n)], inc=(mk.vv, 1))
            p.op("sync", lambda s, g=gtile, slp=slp: s.dma_start(out=xoutD[g * 128:(g + 1) * 128, :], in_=mk.xo[:, slp, :]),
                 wait=(mk.vv, mk.vv.n), inc=(mk.lsto, 16))
            if xlnoutD is not None:
                ln_tile(mk, mk.xo[:, slp, :], mk.lnf0[:, slp, :], (mk.vv, mk.vv.n),
                        store_war=(mk.lsto, mk.lsto.n))
                p.op("sync", lambda s, g=gtile, slp=slp: s.dma_start(
                    out=xlnoutD[g * 128:(g + 1) * 128, :], in_=mk.lnf0[:, slp, :]),
                    wait=(mk.vv, mk.vv.n), inc=(mk.lsto, 16))
    fence_all(mk)


def allgather_stats(mk, statsD, width):
    p = mk.p
    tag = f"ag{mk._agn}" if hasattr(mk, "_agn") else "ag0"
    mk._agn = (getattr(mk, "_agn", 0)) + 1
    outD = mk.nc.dram_tensor(f"{tag}_out", [64, width], F32, kind="Internal").ap()
    p.op("gpsimd", lambda g: g.collective_compute(
        "AllGather", ALU.bypass, replica_groups=[list(range(8))],
        ins=[statsD.opt()], outs=[outD.opt()]),
        wait=[(mk.lsto, mk.lsto.n)], inc=(mk.gp, 1))
    p.wait("sync", mk.gp, mk.gp.n)
    return outD


def sub_F0(mk, xD, xlnD, gwT_D, lbD, expert_ws, statsD, statsAllD, sumselD, xoutD, stats_only,
           collective=False, skip_ln=False, xlnoutD=None):
    mk.phase_f0()
    selbuf = mk.selbuf
    mk.xlt = mk.xlt2048[:, :, 0:512]
    if stats_only:
        fence_all(mk)
        ln_stream(mk, xD, xlnD)
        f0_routing(mk, xlnD, gwT_D, lbD, selbuf, statsD)
        fence_all(mk)
        return
    fence_all(mk)
    if not skip_ln:
        ln_stream(mk, xD, xlnD)
    f0_routing(mk, xlnD, gwT_D, lbD, selbuf, statsD)
    if collective:
        statsAllD = allgather_stats(mk, statsD, 2)
    wmean_bc = f0_wmean_from_stats(mk, statsAllD, sumselD)
    fence_all(mk)
    f0_experts(mk, xD, xlnD, selbuf, wmean_bc, expert_ws, xoutD, mk.xlt2048, mk.acc, mk.hbuf,
               xlnoutD=xlnoutD)



def f1_routing(mk, xlnD, routerT_D, ohbuf, pcolb, cntout_cb):
    p = mk.p
    c = mk.cols
    if not hasattr(mk, "rtw"):
        mk.rtw = p.sb("b_rtw", [128, 8, 8], BF16)
    mk.load_wT(mk.rtw, routerT_D, 64, wait=[(mk.mm, mk.mm.n), (mk.sc, mk.sc.n)])
    fence_all(mk)
    cur = load_xlt(mk, xlnD, 0, 0, 0)
    for gb in range(mk.NB512):
        xl, sem, ld_n = cur
        if gb + 1 < mk.NB512:
            seq1, blk1 = divmod(gb + 1, mk.TB512)
            cur = load_xlt(mk, xlnD, seq1, blk1, gb + 1)
        for t4 in range(4):
            gtile = gb * 4 + t4
            for ci in range(8):
                p.op("tensor", lambda t, ci=ci, t4=t4, xl=xl: t.matmul(
                    mk.ps[0][:, 0:8], xl[:, ci, t4 * 128:(t4 + 1) * 128], mk.rtw[:, ci, 0:8],
                    start=(ci == 0), stop=(ci == 7)),
                    wait=[(sem, ld_n), (mk.vv, mk.vv.n), (mk.sc, mk.sc.n)] if ci == 0 else None,
                    inc=(mk.mm, 1) if ci == 7 else None)
            mm_n = mk.mm.n
            p.op("vector", lambda v: v.reduce_max(c[:, 0:1], mk.ps[0][:, 0:8], axis=AX.X),
                 wait=(mk.mm, mm_n), inc=(mk.vv, 1))
            p.op("vector", lambda v: v.tensor_scalar(c[:, 2:3], c[:, 0:1], -1.0, None, ALU.mult),
                 wait=(mk.vv, mk.vv.n), inc=(mk.vv, 1))
            p.op("scalar", lambda s: s.activation(out=mk.xsq[:, 0:8], in_=mk.ps[0][:, 0:8], func=AF.Exp,
                                                  bias=c[:, 2:3], accum_out=c[:, 3:4]),
                 wait=[(mk.vv, mk.vv.n), (mk.sc, mk.sc.n)], inc=(mk.sc, 1))
            p.op("vector", lambda v, gtile=gtile: v.reciprocal(pcolb[:, gtile:gtile + 1], c[:, 3:4]),
                 wait=(mk.sc, mk.sc.n), inc=(mk.vv, 1))
            p.op("vector", lambda v, gtile=gtile: v.tensor_scalar(
                ohbuf[:, gtile, :], mk.ps[0][:, 0:8], c[:, 0:1], None, ALU.is_equal),
                 wait=(mk.vv, mk.vv.n), inc=(mk.vv, 1))
    acc = mk.sq2[:, 1024:1032]
    p.op("vector", lambda v: v.memset(acc, 0.0), wait=(mk.vv, mk.vv.n), inc=(mk.vv, 1))
    for gtile in range(mk.NT):
        p.op("vector", lambda v, gtile=gtile: v.tensor_tensor(acc, acc, ohbuf[:, gtile, :], ALU.add),
             wait=(mk.vv, mk.vv.n), inc=(mk.vv, 1))
    p.op("tensor", lambda t: t.matmul(mk.ps[1][0:8, 0:128], acc, mk.ident32[:, :],
                                      start=True, stop=True), wait=(mk.vv, mk.vv.n), inc=(mk.mm, 1))
    p.op("vector", lambda v: v.reduce_sum(c[0:8, 15:16], mk.ps[1][0:8, 0:128], axis=AX.X),
         wait=(mk.mm, mk.mm.n), inc=(mk.vv, 1))
    cntout_cb()


def sub_F1(mk, xD, xlnD, routerT_D, w1D, b1D, w2D, b2D, cntD, cntAllD, prefselD,
           xoutD, stats_only, tag="f1", collective=False, skip_ln=False, xlnoutD=None):
    p = mk.p
    c = mk.cols
    import math
    C_CAP = math.ceil(1.5 * mk.nglobal / 8)
    NBLK = (mk.N + 8 * 511 + 511) // 512
    ohbuf = mk.ohbuf
    pcolb = mk.pcolb

    if stats_only:
        fence_all(mk)
        ln_stream(mk, xD, xlnD)
        f1_routing(mk, xlnD, routerT_D, ohbuf, pcolb,
                   lambda: p.op("sync", lambda s: s.dma_start(out=cntD, in_=c[0:8, 15:16]),
                                wait=(mk.vv, mk.vv.n), inc=(mk.lsto, 16)))
        fence_all(mk)
        return
    mk.phase_f1()
    scaleb = mk.scaleb
    su32 = mk.su32
    ohTc = mk.ohTc
    w1g = mk.w1g
    w2g = mk.w2g
    hT = mk.hT
    bb = mk.bb
    bbu = mk.bbu
    b1c = mk.b1c
    b2r = mk.b2r
    xlnt = mk.xln
    ysr = mk.xo[:, 1, :]
    NBLK_ = NBLK
    xsortD = mk.nc.dram_tensor(tag + "_xsort", [NBLK_ * 512, 1024], BF16, kind="Internal").ap()
    ysortD = mk.nc.dram_tensor(tag + "_ysort", [NBLK_ * 512, 1024], F32, kind="Internal").ap()
    b1TD = mk.nc.dram_tensor(tag + "_b1T", [1024, 32], BF16, kind="Internal").ap()
    b1bfD = mk.nc.dram_tensor(tag + "_b1bf", [8, 32, 128], BF16, kind="Internal").ap()
    b2bfD = mk.nc.dram_tensor(tag + "_b2bf", [16, 512], BF16, kind="Internal").ap()
    p.op("sync", lambda s: s.dma_start(out=mk.xt[:, 0, 0:256],
                                       in_=b1D.rearrange("e f -> (e f)").rearrange("(p f) -> p f", p=128)),
         inc=(mk.lsb, 16))
    p.op("vector", lambda v: v.tensor_copy(xlnt[:, 0, 0:256], mk.xt[:, 0, 0:256]),
         wait=(mk.lsb, mk.lsb.n), inc=(mk.vv, 1))
    p.op("sync", lambda s: s.dma_start(out=b1bfD.rearrange("e c p -> (e c p)").rearrange("(p f) -> p f", p=128),
                                       in_=xlnt[:, 0, 0:256]), wait=(mk.vv, mk.vv.n), inc=(mk.lsto, 16))
    p.op("sync", lambda s: s.dma_start(out=mk.xt[:, 0, 0:64],
                                       in_=b2D.rearrange("e f -> (e f)").rearrange("(p f) -> p f", p=128)),
         inc=(mk.lsb, 16))
    p.op("vector", lambda v: v.tensor_copy(xlnt[:, 0, 256:320], mk.xt[:, 0, 0:64]),
         wait=[(mk.lsb, mk.lsb.n), (mk.vv, mk.vv.n)], inc=(mk.vv, 1))
    p.op("sync", lambda s: s.dma_start(out=b2bfD.rearrange("a b -> (a b)").rearrange("(p f) -> p f", p=128),
                                       in_=xlnt[:, 0, 256:320]),
         wait=(mk.vv, mk.vv.n), inc=(mk.lsto, 16))
    for e in range(8):
        p.op("sync", lambda s, e=e: s.dma_start(out=b1c[:, :], in_=b1bfD[e], transpose=True),
             wait=(mk.lsto, mk.lsto.n), inc=(mk.lst, 16))
        p.op("sync", lambda s, e=e: s.dma_start(out=b1TD[e * 128:(e + 1) * 128, :], in_=b1c[:, :]),
             wait=(mk.lst, mk.lst.n), inc=(mk.lsto, 16))
    fence_all(mk)
    if not skip_ln:
        ln_stream(mk, xD, xlnD)
    if collective:
        f1_routing(mk, xlnD, routerT_D, ohbuf, pcolb,
                   lambda: p.op("sync", lambda s: s.dma_start(out=cntD, in_=c[0:8, 15:16]),
                                wait=(mk.vv, mk.vv.n), inc=(mk.lsto, 16)))
        cntAllD = allgather_stats(mk, cntD, 1)
    else:
        f1_routing(mk, xlnD, routerT_D, ohbuf, pcolb, lambda: None)
    p.op("gpsimd", lambda g: g.iota(bb[:, 0:8], pattern=[[128, 8]], base=0, channel_multiplier=1,
                                    allow_small_or_imprecise_dtypes=True), wait=(mk.gp, mk.gp.n), inc=(mk.gp, 1))
    p.op("gpsimd", lambda g: g.iota(bb[:, 8:40], pattern=[[128, 32]], base=0, channel_multiplier=1,
                                    allow_small_or_imprecise_dtypes=True), wait=(mk.gp, mk.gp.n), inc=(mk.gp, 1))
    p.op("gpsimd", lambda g: g.iota(bb[:, 40:41], pattern=[[1, 1]], base=0, channel_multiplier=1,
                                    allow_small_or_imprecise_dtypes=True), wait=(mk.gp, mk.gp.n), inc=(mk.gp, 1))
    p.op("gpsimd", lambda g: g.iota(bb[:, 41:42], pattern=[[1, 1]], base=0, channel_multiplier=512,
                                    allow_small_or_imprecise_dtypes=True), wait=(mk.gp, mk.gp.n), inc=(mk.gp, 1))
    p.op("gpsimd", lambda g: g.affine_select(mk.srow[0:8, 40:48], mk.ones32[0:8, 0:8], pattern=[[1, 8]],
                                             compare_op=ALU.is_ge, fill=0.0, base=-1, channel_multiplier=-1),
         wait=(mk.gp, mk.gp.n), inc=(mk.gp, 1))
    p.op("sync", lambda s: s.dma_start(out=mk.srow[0:64, 16:17], in_=cntAllD), inc=(mk.lsb, 16))
    p.op("sync", lambda s: s.dma_start(out=mk.srow[0:64, 17:25], in_=prefselD), inc=(mk.lsb, 16))
    p.op("tensor", lambda t: t.matmul(mk.ps[0][0:8, 0:1], mk.srow[0:64, 17:25], mk.srow[0:64, 16:17],
                                      start=True, stop=True),
         wait=[(mk.lsb, mk.lsb.n), (mk.vv, mk.vv.n), (mk.gp, mk.gp.n)], inc=(mk.mm, 1))
    p.op("vector", lambda v: v.tensor_copy(c[0:8, 8:9], mk.ps[0][0:8, 0:1]),
         wait=(mk.mm, mk.mm.n), inc=(mk.vv, 1))
    p.op("vector", lambda v: v.tensor_scalar(c[0:8, 9:10], c[0:8, 15:16], 255.5, 1.0 / 512, ALU.add, ALU.mult),
         wait=(mk.vv, mk.vv.n), inc=(mk.vv, 1))
    p.op("vector", lambda v: v.tensor_scalar(c[0:8, 10:11], c[0:8, 9:10], 8388608.0, -8388608.0, ALU.add, ALU.add),
         wait=(mk.vv, mk.vv.n), inc=(mk.vv, 1))
    p.op("vector", lambda v: v.tensor_scalar(c[0:8, 11:12], c[0:8, 10:11], 512.0, None, ALU.mult),
         wait=(mk.vv, mk.vv.n), inc=(mk.vv, 1))
    p.op("tensor", lambda t: t.matmul(mk.ps[0][0:8, 0:1], mk.srow[0:8, 40:48], c[0:8, 11:12],
                                      start=True, stop=True), wait=(mk.vv, mk.vv.n), inc=(mk.mm, 1))
    p.op("vector", lambda v: v.tensor_copy(c[0:8, 12:13], mk.ps[0][0:8, 0:1]),
         wait=(mk.mm, mk.mm.n), inc=(mk.vv, 1))
    p.op("vector", lambda v: v.tensor_tensor(c[0:8, 13:14], c[0:8, 12:13], c[0:8, 8:9], ALU.subtract),
         wait=(mk.vv, mk.vv.n), inc=(mk.vv, 1))
    p.op("vector", lambda v: v.tensor_scalar(c[0:8, 13:14], c[0:8, 13:14], -1.0, None, ALU.add),
         wait=(mk.vv, mk.vv.n), inc=(mk.vv, 1))
    p.op("tensor", lambda t: t.matmul(mk.ps[0][0:1, 0:8], c[0:8, 13:14], mk.ident32[0:8, 0:8],
                                      start=True, stop=True), wait=(mk.vv, mk.vv.n), inc=(mk.mm, 1))
    p.op("vector", lambda v: v.tensor_copy(mk.srow[0:1, 48:56], mk.ps[0][0:1, 0:8]),
         wait=(mk.mm, mk.mm.n), inc=(mk.vv, 1))
    p.op("tensor", lambda t: t.matmul(mk.ps[0][0:1, 0:8], c[0:8, 12:13], mk.ident32[0:8, 0:8],
                                      start=True, stop=True), wait=(mk.vv, mk.vv.n), inc=(mk.mm, 1))
    p.op("vector", lambda v: v.tensor_copy(mk.srow[0:1, 56:64], mk.ps[0][0:1, 0:8]),
         wait=(mk.mm, mk.mm.n), inc=(mk.vv, 1))
    p.op("tensor", lambda t: t.matmul(mk.ps[0][:, 0:8], mk.ones32[0:1, 0:128], mk.srow[0:1, 48:56],
                                      start=True, stop=True), wait=(mk.vv, mk.vv.n), inc=(mk.mm, 1))
    adjbc = mk.sq2[:, 1032:1040]
    p.op("vector", lambda v: v.tensor_copy(adjbc, mk.ps[0][:, 0:8]), wait=(mk.mm, mk.mm.n), inc=(mk.vv, 1))
    p.op("tensor", lambda t: t.matmul(mk.ps[0][:, 0:8], mk.ones32[0:1, 0:128], mk.srow[0:1, 56:64],
                                      start=True, stop=True), wait=(mk.vv, mk.vv.n), inc=(mk.mm, 1))
    stbc = mk.sq2[:, 1040:1048]
    p.op("vector", lambda v: v.tensor_copy(stbc, mk.ps[0][:, 0:8]), wait=(mk.mm, mk.mm.n), inc=(mk.vv, 1))
    p.op("vector", lambda v: v.tensor_scalar(mk.sq2[:, 0:8], stbc, bb[:, 41:42], None, ALU.is_le),
         wait=(mk.vv, mk.vv.n), inc=(mk.vv, 1))
    p.op("vector", lambda v: v.reduce_sum(c[:, 4:5], mk.sq2[:, 0:8], axis=AX.X),
         wait=(mk.vv, mk.vv.n), inc=(mk.vv, 1))
    p.op("vector", lambda v: v.tensor_scalar(c[:, 4:5], c[:, 4:5], -1.0, None, ALU.add),
         wait=(mk.vv, mk.vv.n), inc=(mk.vv, 1))
    p.op("vector", lambda v: v.tensor_scalar(mk.sq2[:, 0:8], stbc, bb[:, 41:42], None, ALU.is_equal),
         wait=(mk.vv, mk.vv.n), inc=(mk.vv, 1))
    p.op("vector", lambda v: v.reduce_sum(c[:, 5:6], mk.sq2[:, 0:8], axis=AX.X),
         wait=(mk.vv, mk.vv.n), inc=(mk.vv, 1))
    p.op("vector", lambda v: v.tensor_scalar(c[:, 5:6], c[:, 5:6], 1.0, None, ALU.min),
         wait=(mk.vv, mk.vv.n), inc=(mk.vv, 1))
    p.op("vector", lambda v: v.tensor_scalar(c[:, 6:7], c[:, 5:6], -1.0, 1.0, ALU.mult, ALU.add),
         wait=(mk.vv, mk.vv.n), inc=(mk.vv, 1))
    p.op("vector", lambda v: v.tensor_copy(c[:, 7:8], c[:, 4:5]),
         wait=(mk.vv, mk.vv.n), inc=(mk.vv, 1))
    p.op("tensor", lambda t: t.matmul(mk.ps[0][0:1, 0:128], c[:, 7:8], mk.ident32[:, :],
                                      start=True, stop=True), wait=(mk.vv, mk.vv.n), inc=(mk.mm, 1))
    p.op("vector", lambda v: v.tensor_copy(mk.zrow[0:1, 1, 0:128], mk.ps[0][0:1, 0:128]),
         wait=(mk.mm, mk.mm.n), inc=(mk.vv, 1))
    NCC = (mk.N + 1023) // 1024
    for cc in range(NCC):
        ntile_c = min(8, mk.NT - cc * 8)
        for grp in range((ntile_c + 3) // 4):
            for t4 in range(min(4, ntile_c - grp * 4)):
                gtile = cc * 8 + grp * 4 + t4
                p.op("tensor", lambda t, gtile=gtile, grp=grp, t4=t4: t.matmul(
                    mk.ps[1][0:8, grp * 0 + t4 * 128:(t4 + 1) * 128], ohbuf[:, gtile, :], mk.ident[:, :],
                    start=True, stop=True, skip_group_check=True),
                    wait=(mk.vv, mk.vv.n) if t4 == 0 else None, inc=(mk.mm, 1))
            mm_n = mk.mm.n
            p.op("vector", lambda v, grp=grp, ntile_c=ntile_c: v.tensor_copy(
                ohTc[0:8, grp * 512:grp * 512 + min(4, ntile_c - grp * 4) * 128],
                mk.ps[1][0:8, 0:min(4, ntile_c - grp * 4) * 128]),
                wait=(mk.mm, mm_n), inc=(mk.vv, 1))
        init = c[0:8, 8:9] if cc == 0 else c[0:8, 14:15]
        p.op("vector", lambda v, init=init, ntile_c=ntile_c: v.tensor_tensor_scan(
            ohTc[0:8, 0:ntile_c * 128], ohTc[0:8, 0:ntile_c * 128], ohTc[0:8, 0:ntile_c * 128],
            init, ALU.add, ALU.max),
            wait=(mk.vv, mk.vv.n), inc=(mk.vv, 1))
        p.op("vector", lambda v, ntile_c=ntile_c: v.tensor_copy(c[0:8, 14:15], ohTc[0:8, ntile_c * 128 - 1:ntile_c * 128]),
             wait=(mk.vv, mk.vv.n), inc=(mk.vv, 1))
        for lt in range(ntile_c):
            gtile = cc * 8 + lt
            p.op("tensor", lambda t, lt=lt: t.matmul(mk.ps[2][:, 0:8], ohTc[0:8, lt * 128:(lt + 1) * 128],
                                                     mk.ident32[0:8, 0:8], start=True, stop=True),
                 wait=(mk.vv, mk.vv.n), inc=(mk.mm, 1))
            p.op("scalar", lambda s: s.activation(out=mk.xsq[:, 0:8], in_=mk.ps[2][:, 0:8], func=AF.Identity),
                 wait=[(mk.mm, mk.mm.n), (mk.vv, mk.vv.n), (mk.sc, mk.sc.n)], inc=(mk.sc, 1))
            p.op("vector", lambda v, gtile=gtile: v.tensor_tensor(mk.xsq[:, 8:16], mk.xsq[:, 0:8],
                                                                  ohbuf[:, gtile, :], ALU.mult),
                 wait=(mk.sc, mk.sc.n), inc=(mk.vv, 1))
            p.op("vector", lambda v: v.reduce_sum(c[:, 0:1], mk.xsq[:, 8:16], axis=AX.X),
                 wait=(mk.vv, mk.vv.n), inc=(mk.vv, 1))
            p.op("vector", lambda v, C_CAP=C_CAP: v.tensor_scalar(c[:, 1:2], c[:, 0:1], float(C_CAP), None, ALU.is_le),
                 wait=(mk.vv, mk.vv.n), inc=(mk.vv, 1))
            p.op("vector", lambda v, gtile=gtile: v.tensor_tensor(scaleb[:, gtile:gtile + 1], c[:, 1:2],
                                                                  pcolb[:, gtile:gtile + 1], ALU.mult),
                 wait=(mk.vv, mk.vv.n), inc=(mk.vv, 1))
            p.op("vector", lambda v: v.tensor_tensor(mk.xsq[:, 0:8], mk.xsq[:, 0:8], adjbc, ALU.add),
                 wait=(mk.vv, mk.vv.n), inc=(mk.vv, 1))
            p.op("vector", lambda v, gtile=gtile: v.tensor_tensor(mk.xsq[:, 0:8], mk.xsq[:, 0:8],
                                                                  ohbuf[:, gtile, :], ALU.mult),
                 wait=(mk.vv, mk.vv.n), inc=(mk.vv, 1))
            p.op("vector", lambda v: v.reduce_sum(c[:, 2:3], mk.xsq[:, 0:8], axis=AX.X),
                 wait=(mk.vv, mk.vv.n), inc=(mk.vv, 1))
            p.op("vector", lambda v, gtile=gtile: v.tensor_copy(su32[:, gtile:gtile + 1], c[:, 2:3]),
                 wait=(mk.vv, mk.vv.n), inc=(mk.vv, 1))
    zt = xlnt[:, 0, :]
    p.op("gpsimd", lambda g: g.memset(zt, 0.0), wait=(mk.ld, mk.ld.n), inc=(mk.gp, 1))
    for r in range(NBLK * 4):
        p.op("sync", lambda s, r=r: s.dma_start(out=xsortD[r * 128:(r + 1) * 128, :], in_=zt),
             wait=(mk.gp, mk.gp.n) if r == 0 else None, inc=(mk.lsto, 16))
    zs_done = mk.lsto.n
    for gtile in range(mk.NT):
        sl = gtile % 2
        p.op("sync", lambda s, gtile=gtile, sl=sl: s.dma_start(out=xlnt[:, sl, :],
                                                               in_=xlnD[gtile * 128:(gtile + 1) * 128, :]),
             wait=[(mk.ld, mk.ld.n), (mk.lsto, zs_done)], inc=(mk.lsx[sl], 16))
        p.op("gpsimd", lambda g, gtile=gtile, sl=sl: g.indirect_dma_start(
            out=xsortD, out_offset=bass.IndirectOffsetOnAxis(ap=su32[:, gtile:gtile + 1], axis=0),
            in_=xlnt[:, sl, :], in_offset=None),
            wait=[(mk.lsx[sl], mk.lsx[sl].n), (mk.vv, mk.vv.n), (mk.lsto, zs_done)], inc=(mk.ld, 16))
    ld_scatter = mk.ld.n
    sc_pb = 0
    for b in range(NBLK):
        p.op("tensor", lambda t, b=b: t.matmul(mk.ps[5][:, 0:1], mk.ones32[0:1, 0:128],
                                               mk.zrow[0:1, 1, b:b + 1], start=True, stop=True),
             wait=[(mk.vv, mk.vv.n), (mk.sc, mk.sc.n)], inc=(mk.mm, 1))
        p.op("vector", lambda v: v.tensor_copy(c[:, 8:9], mk.ps[5][:, 0:1]), wait=(mk.mm, mk.mm.n), inc=(mk.vv, 1))
        for (dst, bcsl, stride) in (((0, 8), (0, 8), 1024.0), ((8, 40), (8, 40), 4096.0),
                                    ((40, 41), (40, 41), 128.0), ((41, 42), (40, 41), 2.0)):
            p.op("vector", lambda v, stride=stride: v.tensor_scalar(c[:, 9:10], c[:, 8:9], stride, None, ALU.mult),
                 wait=(mk.vv, mk.vv.n), inc=(mk.vv, 1))
            p.op("vector", lambda v, dst=dst, bcsl=bcsl: v.tensor_scalar(
                mk.sq2[:, dst[0]:dst[1]], bb[:, bcsl[0]:bcsl[1]], c[:, 9:10], None, ALU.add),
                 wait=(mk.vv, mk.vv.n), inc=(mk.vv, 1))
            p.op("vector", lambda v, dst=dst: v.tensor_copy(bbu[:, dst[0]:dst[1]], mk.sq2[:, dst[0]:dst[1]]),
                 wait=(mk.vv, mk.vv.n), inc=(mk.vv, 1))
        vv_idx = mk.vv.n
        p.wait("gpsimd", mk.mm, mk.mm.n)
        p.wait("gpsimd", mk.vv, vv_idx)
        ld_w1 = []
        for ci in range(8):
            for hf in range(2):
                p.op("gpsimd", lambda g, ci=ci, hf=hf: g.indirect_dma_start(
                    out=w1g[:, ci, hf * 2048:(hf + 1) * 2048],
                    out_offset=None, in_=w1D,
                    in_offset=bass.IndirectOffsetOnAxis(ap=bbu[:, ci:ci + 1], axis=0),
                    element_offset=hf * 2048), inc=(mk.ld, 16))
            ld_w1.append(mk.ld.n)
        p.op("gpsimd", lambda g: g.indirect_dma_start(
            out=b1c[:, :], out_offset=None, in_=b1TD,
            in_offset=bass.IndirectOffsetOnAxis(ap=bbu[:, 40:41], axis=0)), inc=(mk.ld, 16))
        p.op("gpsimd", lambda g: g.indirect_dma_start(
            out=b2r[0:65:64, :], out_offset=None, in_=b2bfD,
            in_offset=bass.IndirectOffsetOnAxis(ap=bbu[0:2, 41:42], axis=0)), inc=(mk.ld, 16))
        ld_b = mk.ld.n
        ld_w2 = []
        for hf in range(32):
            p.op("gpsimd", lambda g, hf=hf: g.indirect_dma_start(
                out=w2g[:, hf, :], out_offset=None, in_=w2D,
                in_offset=bass.IndirectOffsetOnAxis(ap=bbu[:, 8 + hf:9 + hf], axis=0)), inc=(mk.ld, 16))
            ld_w2.append(mk.ld.n)
        p.wait("sync", mk.ld, ld_scatter)
        for tt in range(4):
            p.op("sync", lambda s, b=b, tt=tt, sl=tt % 2: s.dma_start(
                out=xlnt[:, sl, :], in_=xsortD[b * 512 + tt * 128:b * 512 + (tt + 1) * 128, :]),
                wait=(mk.mm, mk.mm.n), inc=(mk.lst, 16))
            ld_t = mk.lst.n
            for ci in range(8):
                p.op("tensor", lambda t, ci=ci, sl=tt % 2: t.matmul(
                    mk.ps[0][:, 0:128], xlnt[:, sl, ci * 128:(ci + 1) * 128], mk.ident[:, :],
                    start=True, stop=True),
                    wait=[(mk.lst, ld_t), (mk.sc, mk.sc.n)] if ci == 0 else (mk.sc, mk.sc.n), inc=(mk.mm, 1))
                p.op("scalar", lambda s, ci=ci, tt=tt: s.activation(
                    out=mk.xlt[:, ci, tt * 128:(tt + 1) * 128], in_=mk.ps[0][:, 0:128],
                    func=AF.Identity), wait=(mk.mm, mk.mm.n), inc=(mk.sc, 1))
        for sub in range(2):
            for fo in range(32):
                for ci in range(8):
                    ldw_ci = (mk.ld, ld_w1[ci]) if (sub == 0 and fo == 0) else None
                    sc_w = (mk.sc, mk.sc.n) if ci == 0 else None
                    waits = [w for w in (ldw_ci, sc_w) if w]
                    p.op("tensor", lambda t, fo=fo, ci=ci, sub=sub: t.matmul(
                        mk.ps[1][:, 0:256], w1g[:, ci, fo * 128:(fo + 1) * 128],
                        mk.xlt[:, ci, sub * 256:(sub + 1) * 256],
                        start=(ci == 0), stop=(ci == 7)),
                        wait=waits or None,
                        inc=(mk.mm, 1) if ci == 7 else None)
                p.op("scalar", lambda s, fo=fo: s.activation(out=hT[:, fo, :], in_=mk.ps[1][:, 0:256],
                                                             func=AF.Gelu, bias=b1c[:, fo:fo + 1]),
                     wait=[(mk.mm, mk.mm.n)] + ([(mk.ld, ld_b)] if (sub == 0 and fo == 0) else []),
                     inc=(mk.sc, 1))
            for t2 in range(2):
                tt = sub * 2 + t2
                for oh in range(2):
                    bank = mk.ps[2 + oh]
                    for hf in range(32):
                        ldw_hf = (mk.ld, ld_w2[hf]) if (sub == 0 and t2 == 0 and oh == 0) else None
                        sc_w = (mk.sc, mk.sc.n) if hf == 0 else None
                        waits = [w for w in (ldw_hf, sc_w) if w]
                        p.op("tensor", lambda t, hf=hf, t2=t2, oh=oh, bank=bank: t.matmul(
                            bank[:, :], hT[:, hf, t2 * 128:(t2 + 1) * 128], w2g[:, hf, oh * 512:(oh + 1) * 512],
                            start=(hf == 0), stop=False),
                            wait=waits or None)
                    p.op("tensor", lambda t, oh=oh, bank=bank: t.matmul(
                        bank[:, :], mk.ones_bf[0 if oh == 0 else 64:1 if oh == 0 else 65, :],
                        b2r[0 if oh == 0 else 64:1 if oh == 0 else 65, :], start=False, stop=True),
                        inc=(mk.mm, 1))
                mm_y = mk.mm.n
                for oh in range(2):
                    p.op("scalar", lambda s, oh=oh: s.activation(out=ysr[:, oh * 512:(oh + 1) * 512],
                                                                 in_=mk.ps[2 + oh][:, :], func=AF.Identity),
                         wait=[(mk.mm, mm_y), (mk.lsto, mk.lsto.n)], inc=(mk.sc, 1))
                p.op("sync", lambda s, b=b, tt=tt: s.dma_start(
                    out=ysortD[b * 512 + tt * 128:b * 512 + (tt + 1) * 128, :], in_=ysr[:, :]),
                    wait=(mk.sc, mk.sc.n), inc=(mk.lsto, 16))
    ys_done = mk.lsto.n
    for gtile in range(mk.NT):
        sl = gtile % 2
        p.op("gpsimd", lambda g, gtile=gtile, sl=sl: g.indirect_dma_start(
            out=mk.xo[:, sl, :], out_offset=None, in_=ysortD,
            in_offset=bass.IndirectOffsetOnAxis(ap=su32[:, gtile:gtile + 1], axis=0)),
            wait=[(mk.lsto, ys_done), (mk.vv, mk.vv.n)], inc=(mk.ld, 16))
        p.op("sync", lambda s, gtile=gtile, sl=sl: s.dma_start(out=mk.xt[:, sl, :],
                                                              in_=xD[gtile * 128:(gtile + 1) * 128, :]),
             wait=(mk.vv, mk.vv.n), inc=(mk.lsx[sl], 16))
        p.op("vector", lambda v, gtile=gtile, sl=sl: v.scalar_tensor_tensor(
            mk.xt[:, sl, :], mk.xo[:, sl, :], scaleb[:, gtile:gtile + 1], mk.xt[:, sl, :],
            ALU.mult, ALU.add),
            wait=[(mk.ld, mk.ld.n), (mk.lsx[sl], mk.lsx[sl].n), (mk.lsto, mk.lsto.n)], inc=(mk.vv, 1))
        p.op("sync", lambda s, gtile=gtile, sl=sl: s.dma_start(out=xoutD[gtile * 128:(gtile + 1) * 128, :],
                                                              in_=mk.xt[:, sl, :]),
             wait=(mk.vv, mk.vv.n), inc=(mk.lsto, 16))
        if xlnoutD is not None:
            ln_tile(mk, mk.xt[:, sl, :], mk.lnf1[:, sl, :], (mk.vv, mk.vv.n),
                    store_war=(mk.lsto, mk.lsto.n))
            p.op("sync", lambda s, gtile=gtile, sl=sl: s.dma_start(
                out=xlnoutD[gtile * 128:(gtile + 1) * 128, :], in_=mk.lnf1[:, sl, :]),
                wait=(mk.vv, mk.vv.n), inc=(mk.lsto, 16))
    fence_all(mk)



def _dram_in(nc, name, shape):
    return nc.dram_tensor(name, list(shape), F32, kind="ExternalInput").ap()


def _mk(nc):
    m = MK(nc, B_CORE, T)
    m.nglobal = NGLOBAL
    return m


def build_A0a():
    nc = bass.Bass("TRN2", target_bir_lowering=False, debug=False)
    x = _dram_in(nc, "x", (N, 1024))
    wbT = _dram_in(nc, "wbT", (1024, 64))
    wvT = _dram_in(nc, "wvT", (1024, 1024))
    bv = _dram_in(nc, "bv", (1024,))
    woT = _dram_in(nc, "woT", (1024, 1024))
    bo = _dram_in(nc, "bo", (1024,))
    xln = nc.dram_tensor("xln", [N, 1024], BF16, kind="Internal").ap()
    xout = nc.dram_tensor("xout", [N, 1024], F32, kind="ExternalOutput").ap()
    mk = _mk(nc)
    sub_A0a(mk, x, xln, wbT, wvT, bv, woT, bo, xout)
    return mk.p.build()


def build_A0b():
    nc = bass.Bass("TRN2", target_bir_lowering=False, debug=False)
    x = _dram_in(nc, "x", (N, 1024))
    wq = _dram_in(nc, "wqT", (1024, 1024)); bq = _dram_in(nc, "bq", (1024,))
    wk = _dram_in(nc, "wkT", (1024, 1024)); bk = _dram_in(nc, "bk", (1024,))
    wv = _dram_in(nc, "wvT", (1024, 1024)); bv = _dram_in(nc, "bv", (1024,))
    wo = _dram_in(nc, "woT", (1024, 1024)); bo = _dram_in(nc, "bo", (1024,))
    xln = nc.dram_tensor("xln", [N, 1024], BF16, kind="Internal").ap()
    xout = nc.dram_tensor("xout", [N, 1024], F32, kind="ExternalOutput").ap()
    mk = _mk(nc)
    sub_A0b(mk, x, xln, wq, bq, wk, bk, wv, bv, wo, bo, xout)
    return mk.p.build()


def build_A1a():
    nc = bass.Bass("TRN2", target_bir_lowering=False, debug=False)
    x = _dram_in(nc, "x", (N, 1024))
    wkvd = _dram_in(nc, "wkvd", (1024, 128))
    wkvu = _dram_in(nc, "wkvu", (128, 2048))
    wq = _dram_in(nc, "wq", (1024, 1024))
    wo = _dram_in(nc, "wo", (1024, 1024))
    xln = nc.dram_tensor("xln", [N, 1024], BF16, kind="Internal").ap()
    xout = nc.dram_tensor("xout", [N, 1024], F32, kind="ExternalOutput").ap()
    mk = _mk(nc)
    sub_A1a(mk, x, xln, wkvd, wkvu, wq, wo, xout)
    return mk.p.build()


def build_A1b():
    nc = bass.Bass("TRN2", target_bir_lowering=False, debug=False)
    x = _dram_in(nc, "x", (N, 1024))
    wq = _dram_in(nc, "wqT", (1024, 1024)); bq = _dram_in(nc, "bq", (1024,))
    wk = _dram_in(nc, "wkT", (1024, 1024)); bk = _dram_in(nc, "bk", (1024,))
    wv = _dram_in(nc, "wvT", (1024, 1024)); bv = _dram_in(nc, "bv", (1024,))
    wo = _dram_in(nc, "woT", (1024, 1024)); bo = _dram_in(nc, "bo", (1024,))
    xln = nc.dram_tensor("xln", [N, 1024], BF16, kind="Internal").ap()
    qrot = nc.dram_tensor("qrot", [T, 1024], BF16, kind="Internal").ap()
    krot = nc.dram_tensor("krot", [T, 1024], BF16, kind="Internal").ap()
    xout = nc.dram_tensor("xout", [N, 1024], F32, kind="ExternalOutput").ap()
    mk = _mk(nc)
    sub_A1b(mk, x, xln, wq, bq, wk, bk, wv, bv, wo, bo, qrot, krot, xout)
    return mk.p.build()


def build_F0(stats_only):
    nc = bass.Bass("TRN2", target_bir_lowering=False, debug=False)
    x = _dram_in(nc, "x", (N, 1024))
    gwT = _dram_in(nc, "gwT", (1024, 8))
    lb = _dram_in(nc, "lb", (1024,))
    xln = nc.dram_tensor("xln", [N, 1024], BF16, kind="Internal").ap()
    stats = nc.dram_tensor("stats", [8, 2], F32, kind="ExternalOutput").ap()
    mk = _mk(nc)
    if stats_only:
        sub_F0(mk, x, xln, gwT, lb, None, stats, None, None, None, True)
    else:
        egwT = _dram_in(nc, "egwT", (10240, 512))
        euwT = _dram_in(nc, "euwT", (10240, 512))
        edwT = _dram_in(nc, "edwT", (5120, 1024))
        statsAll = _dram_in(nc, "statsAll", (64, 2))
        sumsel = _dram_in(nc, "sumsel", (64, 8))
        xout = nc.dram_tensor("xout", [N, 1024], F32, kind="ExternalOutput").ap()
        sub_F0(mk, x, xln, gwT, lb, (egwT, euwT, edwT),
               stats, statsAll, sumsel, xout, False)
    return mk.p.build()


def build_F1(stats_only):
    nc = bass.Bass("TRN2", target_bir_lowering=False, debug=False)
    x = _dram_in(nc, "x", (N, 1024))
    rT = _dram_in(nc, "rT", (1024, 8))
    xln = nc.dram_tensor("xln", [N, 1024], BF16, kind="Internal").ap()
    cnt = nc.dram_tensor("cnt", [8, 1], F32, kind="ExternalOutput").ap()
    mk = _mk(nc)
    if stats_only:
        sub_F1(mk, x, xln, rT, None, None, None, None, cnt, None, None, None, True)
    else:
        w1 = _dram_in(nc, "w1", (8192, 4096))
        b1 = _dram_in(nc, "b1", (8, 4096))
        w2 = _dram_in(nc, "w2", (32768, 1024))
        b2 = _dram_in(nc, "b2", (8, 1024))
        cntAll = _dram_in(nc, "cntAll", (64, 1))
        prefsel = _dram_in(nc, "prefsel", (64, 8))
        xout = nc.dram_tensor("xout", [N, 1024], F32, kind="ExternalOutput").ap()
        sub_F1(mk, x, xln, rT, w1, b1, w2, b2, cnt, cntAll, prefsel, xout, False)
    return mk.p.build()


def build_fused():
    nc = bass.Bass("TRN2", target_bir_lowering=False, debug=False)
    x0 = _dram_in(nc, "x", (N, 1024))
    w = {}
    for nm, shp in (("a0a_wbT", (128, 512)), ("a0a_wvT", (128, 8192)), ("a0a_bv", (1024,)),
                    ("a0a_woT", (128, 8192)), ("a0a_bo", (1024,)),
                    ("f0_gwT", (128, 64)), ("f0_lb", (1024,)),
                    ("f0_guw", (1280, 8192)), ("f0_edw", (1280, 4096)),
                    ("a0b_wqT", (128, 8192)), ("a0b_bq", (1024,)), ("a0b_wkT", (128, 8192)),
                    ("a0b_bk", (1024,)), ("a0b_wvT", (128, 8192)), ("a0b_bv", (1024,)),
                    ("a0b_woT", (128, 8192)), ("a0b_bo", (1024,)),
                    ("a1a_wkvd", (128, 1024)), ("a1a_wkvu", (128, 2048)),
                    ("a1a_wq", (128, 8192)), ("a1a_wo", (128, 8192)),
                    ("f1_rT", (128, 64)), ("f1_w1", (8192, 4096)), ("f1_b1", (8, 4096)),
                    ("f1_w2", (32768, 1024)), ("f1_b2", (8, 1024)),
                    ("a1b_wqT", (128, 8192)), ("a1b_bq", (1024,)), ("a1b_wkT", (128, 8192)),
                    ("a1b_bk", (1024,)), ("a1b_wvT", (128, 8192)), ("a1b_bv", (1024,)),
                    ("a1b_woT", (128, 8192)), ("a1b_bo", (1024,)),
                    ("sumsel", (64, 8)), ("prefsel", (64, 8))):
        w[nm] = _dram_in(nc, nm, shp)
    xi = [nc.dram_tensor(f"xi{i}", [N, 1024], F32, kind="Internal").ap() for i in range(7)]
    xout = nc.dram_tensor("xout", [N, 1024], F32, kind="ExternalOutput").ap()
    xlnA = nc.dram_tensor("xlnA", [N, 1024], BF16, kind="Internal").ap()
    xlnB = nc.dram_tensor("xlnB", [N, 1024], BF16, kind="Internal").ap()
    qrot = nc.dram_tensor("qrot", [T, 1024], BF16, kind="Internal").ap()
    krot = nc.dram_tensor("krot", [T, 1024], BF16, kind="Internal").ap()
    stats = [nc.dram_tensor(f"stats{i}", [8, 2], F32, kind="Internal").ap() for i in range(2)]
    cnts = [nc.dram_tensor(f"cnts{i}", [8, 1], F32, kind="Internal").ap() for i in range(2)]
    mk = _mk(nc)
    xs = [x0] + xi + [xout]
    sub_A0a(mk, xs[0], xlnA, w["a0a_wbT"], w["a0a_wvT"], w["a0a_bv"], w["a0a_woT"], w["a0a_bo"],
            xs[1], xlnoutD=xlnB)
    sub_F0(mk, xs[1], xlnB, w["f0_gwT"], w["f0_lb"], (w["f0_guw"], w["f0_edw"]),
           stats[0], None, w["sumsel"], xs[2], False, collective=True, skip_ln=True, xlnoutD=xlnA)
    sub_A0b(mk, xs[2], xlnA, w["a0b_wqT"], w["a0b_bq"], w["a0b_wkT"], w["a0b_bk"],
            w["a0b_wvT"], w["a0b_bv"], w["a0b_woT"], w["a0b_bo"], xs[3],
            skip_ln=True, xlnoutD=xlnB)
    sub_F0(mk, xs[3], xlnB, w["f0_gwT"], w["f0_lb"], (w["f0_guw"], w["f0_edw"]),
           stats[1], None, w["sumsel"], xs[4], False, collective=True, skip_ln=True, xlnoutD=xlnA)
    sub_A1a(mk, xs[4], xlnA, w["a1a_wkvd"], w["a1a_wkvu"], w["a1a_wq"], w["a1a_wo"], xs[5],
            skip_ln=True, xlnoutD=xlnB)
    sub_F1(mk, xs[5], xlnB, w["f1_rT"], w["f1_w1"], w["f1_b1"], w["f1_w2"], w["f1_b2"],
           cnts[0], None, w["prefsel"], xs[6], False, tag="f1a", collective=True,
           skip_ln=True, xlnoutD=xlnA)
    sub_A1b(mk, xs[6], xlnA, w["a1b_wqT"], w["a1b_bq"], w["a1b_wkT"], w["a1b_bk"],
            w["a1b_wvT"], w["a1b_bv"], w["a1b_woT"], w["a1b_bo"], qrot, krot, xs[7],
            skip_ln=True, xlnoutD=xlnB)
    sub_F1(mk, xs[7], xlnB, w["f1_rT"], w["f1_w1"], w["f1_b1"], w["f1_w2"], w["f1_b2"],
           cnts[1], None, w["prefsel"], xs[8], False, tag="f1b", collective=True,
           skip_ln=True, xlnoutD=None)
    return mk.p.build()



FUSED = True

_NC_CACHE = {}


def _get_nc(key, builder, *a):
    if key not in _NC_CACHE:
        _NC_CACHE[key] = builder(*a)
    return _NC_CACHE[key]


def _run(nc, per_core_maps, outs):
    res = run_bass_kernel_spmd(nc, per_core_maps, core_ids=list(range(NCORES)))
    return [{k: r[k] for k in outs} for r in res.results]


def _t(a):
    return np.ascontiguousarray(np.asarray(a).T.astype(np.float32))


def _pk(wT):
    c = wT.shape[0] // 128
    return np.ascontiguousarray(wT.reshape(c, 128, wT.shape[1]).transpose(1, 0, 2).reshape(128, -1))


def _pkg(gT, uT):
    ga = gT.reshape(8, 128, 512).transpose(1, 0, 2)
    ua = uT.reshape(8, 128, 512).transpose(1, 0, 2)
    return np.ascontiguousarray(np.concatenate([ga, ua], axis=2).reshape(128, 8192))


def kernel(**inp):
    f = {k: np.asarray(v, dtype=np.float32) for k, v in inp.items()}
    x0 = f["L0_attn_a_hidden_states"].reshape(32, T, 1024)
    xs = [np.ascontiguousarray(x0[c * B_CORE:(c + 1) * B_CORE].reshape(N, 1024)) for c in range(NCORES)]

    lb1024 = np.zeros(1024, np.float32)
    lb1024[:8] = f["L0_ffn_lb_bias"]
    sumsel = np.zeros((64, 8), np.float32)
    for r in range(8):
        for e in range(8):
            sumsel[r * 8 + e, e] = 1.0
    prefsel = []
    for j in range(NCORES):
        pm = np.zeros((64, 8), np.float32)
        for r in range(j):
            for e in range(8):
                pm[r * 8 + e, e] = 1.0
        prefsel.append(pm)

    wA0a = {"wbT": _pk(_t(f["L0_attn_a_W_basis"])), "wvT": _pk(_t(f["L0_attn_a_W_v"])),
            "bv": f["L0_attn_a_b_v"], "woT": _pk(_t(f["L0_attn_a_W_o"])), "bo": f["L0_attn_a_b_o"]}
    wA0b = {"wqT": _pk(_t(f["L0_attn_b_Wq"])), "bq": f["L0_attn_b_bq"],
            "wkT": _pk(_t(f["L0_attn_b_Wk"])), "bk": f["L0_attn_b_bk"],
            "wvT": _pk(_t(f["L0_attn_b_Wv"])), "bv": f["L0_attn_b_bv"],
            "woT": _pk(_t(f["L0_attn_b_Wo"])), "bo": f["L0_attn_b_bo"]}
    wF0s = {"gwT": _pk(_t(f["L0_ffn_gate_w"])), "lb": lb1024}
    wF0 = dict(wF0s)
    sgwT = _t(f["L0_ffn_shared_gate_w"])
    suwT = _t(f["L0_ffn_shared_up_w"])
    sdwT = _t(f["L0_ffn_shared_down_w"])
    egT = f["L0_ffn_expert_gate_w"].transpose(0, 2, 1)
    euT = f["L0_ffn_expert_up_w"].transpose(0, 2, 1)
    edT = f["L0_ffn_expert_down_w"].transpose(0, 2, 1)
    gu_list = [_pkg(egT[e], euT[e]) for e in range(8)]
    gu_list.append(_pkg(sgwT[:, 0:512], suwT[:, 0:512]))
    gu_list.append(_pkg(sgwT[:, 512:1024], suwT[:, 512:1024]))
    ed_list = [_pk(edT[e]) for e in range(8)]
    ed_list.append(_pk(np.ascontiguousarray(sdwT[0:512, :])))
    ed_list.append(_pk(np.ascontiguousarray(sdwT[512:1024, :])))
    wF0.update({"guw": np.ascontiguousarray(np.concatenate(gu_list, axis=0)),
                "edw": np.ascontiguousarray(np.concatenate(ed_list, axis=0)),
                "sumsel": sumsel})
    wA1a = {"wkvd": _pk(f["L1_attn_a_w_kv_down"]), "wkvu": f["L1_attn_a_w_kv_up"],
            "wq": _pk(f["L1_attn_a_w_q"]), "wo": _pk(f["L1_attn_a_w_out"])}
    wA1b = {"wqT": _pk(_t(f["L1_attn_b_Wq"])), "bq": f["L1_attn_b_bq"],
            "wkT": _pk(_t(f["L1_attn_b_Wk"])), "bk": f["L1_attn_b_bk"],
            "wvT": _pk(_t(f["L1_attn_b_Wv"])), "bv": f["L1_attn_b_bv"],
            "woT": _pk(_t(f["L1_attn_b_Wo"])), "bo": f["L1_attn_b_bo"]}
    wF1s = {"rT": _pk(_t(f["L1_ffn_router_w"]))}
    wF1 = dict(wF1s)
    wF1.update({"w1": np.ascontiguousarray(f["L1_ffn_w1"].reshape(8192, 4096)),
                "b1": f["L1_ffn_b1"],
                "w2": np.ascontiguousarray(f["L1_ffn_w2"].reshape(32768, 1024)),
                "b2": f["L1_ffn_b2"]})

    def attn(key, builder, ws):
        nc = _get_nc(key, builder)
        maps = [dict(ws, x=xs[c]) for c in range(NCORES)]
        outs = _run(nc, maps, ["xout"])
        for c in range(NCORES):
            xs[c] = outs[c]["xout"]

    def ffn(key_pre, key_main, builder, ws_pre, ws_main, is_f1):
        nc1 = _get_nc(key_pre, builder, True)
        maps = [dict(ws_pre, x=xs[c]) for c in range(NCORES)]
        outs = _run(nc1, maps, ["cnt" if is_f1 else "stats"])
        if is_f1:
            allstats = np.concatenate([outs[c]["cnt"] for c in range(NCORES)], axis=0)
        else:
            allstats = np.concatenate([outs[c]["stats"] for c in range(NCORES)], axis=0)
        nc2 = _get_nc(key_main, builder, False)
        maps = []
        for c in range(NCORES):
            m = dict(ws_main, x=xs[c])
            if is_f1:
                m["cntAll"] = allstats
                m["prefsel"] = prefsel[c]
            else:
                m["statsAll"] = allstats
            maps.append(m)
        outs = _run(nc2, maps, ["xout"])
        for c in range(NCORES):
            xs[c] = outs[c]["xout"]

    if FUSED:
        nc = _get_nc("fused", build_fused)
        base = {}
        for pre, ws in (("a0a", wA0a), ("a0b", wA0b), ("a1a", wA1a), ("a1b", wA1b)):
            for k, v in ws.items():
                base[pre + "_" + ("wvT" if k == "wv" else k)] = v
        base["a1a_wkvd"] = wA1a["wkvd"]; base["a1a_wkvu"] = wA1a["wkvu"]
        base["a1a_wq"] = wA1a["wq"]; base["a1a_wo"] = wA1a["wo"]
        base.pop("a1a_wvTkvd", None)
        base = {k: v for k, v in base.items() if not k.startswith("a1a_wvT")}
        for k in ("gwT", "lb", "guw", "edw"):
            base["f0_" + k] = wF0[k]
        for k in ("rT", "w1", "b1", "w2", "b2"):
            base["f1_" + k] = wF1[k]
        base["sumsel"] = sumsel
        maps = []
        for c in range(NCORES):
            m = dict(base)
            m["x"] = xs[c]
            m["prefsel"] = prefsel[c]
            maps.append(m)
        outs = _run(nc, maps, ["xout"])
        out = np.empty((32, T, 1024), np.float32)
        for c in range(NCORES):
            out[c * B_CORE:(c + 1) * B_CORE] = outs[c]["xout"].reshape(B_CORE, T, 1024)
        return out

    attn("A0a", build_A0a, wA0a)
    ffn("F0p", "F0m", build_F0, wF0s, wF0, False)
    attn("A0b", build_A0b, wA0b)
    ffn("F0p", "F0m", build_F0, wF0s, wF0, False)
    attn("A1a", build_A1a, wA1a)
    ffn("F1p", "F1m", build_F1, wF1s, wF1, True)
    attn("A1b", build_A1b, wA1b)
    ffn("F1p", "F1m", build_F1, wF1s, wF1, True)

    out = np.empty((32, T, 1024), np.float32)
    for c in range(NCORES):
        out[c * B_CORE:(c + 1) * B_CORE] = xs[c].reshape(B_CORE, T, 1024)
    return out
